# version 01
# baseline (speedup 1.0000x reference)
import math
import numpy as np
import ml_dtypes
from contextlib import ExitStack

import concourse.bass as bass
import concourse.mybir as mybir
from concourse.bass_utils import run_bass_kernel_spmd

F32 = mybir.dt.float32
BF16 = mybir.dt.bfloat16
AF = mybir.ActivationFunctionType
ALU = mybir.AluOpType
AX = mybir.AxisListType

S = 4096
D = 2048
H = 5632
T = 1024
NXT = S // 128
NLT = T // 128
DT = D // 128
IT = D // 128
HT = H // 128
NW = 22
WH = 256
DBN = 4
NELEM = H * D
GBIAS = float(D) * float(D) * 1e-5
LNN = math.log(float(D))
NG = IT

SEMS = ("v", "a", "t",
        "dId", "dCs", "dO", "dGF",
        "dR0", "dR1", "dA0", "dA1", "dQ0", "dQ1",
        "dWa", "dWb", "dWc", "dWd", "dWe", "dWf",
        "dF0", "dF1")
DMA_SEMS = tuple(s for s in SEMS if s.startswith("d"))


def build_nc():
    nc = bass.Bass("TRN2", target_bir_lowering=False)

    xb = nc.dram_tensor("xb", [S, D], F32, kind="ExternalInput")
    w1 = nc.dram_tensor("w1", [H, D], F32, kind="ExternalInput")
    w2 = nc.dram_tensor("w2", [H, D], F32, kind="ExternalInput")
    w3 = nc.dram_tensor("w3", [D, H], F32, kind="ExternalInput")
    ident = nc.dram_tensor("ident", [128, 128], BF16, kind="ExternalInput")
    consts = nc.dram_tensor("consts", [128, 2], F32, kind="ExternalInput")
    out = nc.dram_tensor("out", [T, D], F32, kind="ExternalOutput")
    q1 = nc.dram_tensor("q1", [H, D], BF16, kind="Internal")
    q2 = nc.dram_tensor("q2", [H, D], BF16, kind="Internal")
    q3 = nc.dram_tensor("q3", [D, H], BF16, kind="Internal")
    gscr = nc.dram_tensor("gscr", [3, 128], F32, kind="Internal")
    qs = [q1, q2, q3]

    streams = {e: [] for e in ("sync", "vector", "scalar", "tensor", "gpsimd")}
    VAL = {s: 0 for s in SEMS}

    def em(engine, fn):
        streams[engine].append(fn)

    def iv(s, amt=1):
        VAL[s] += amt
        return VAL[s]

    stacks = []

    def open_stack():
        st = ExitStack()
        stacks.append(st)
        return st

    top = open_stack()
    sem = {n: top.enter_context(nc.semaphore(n)) for n in SEMS}

    def W(engine, s, val):
        if val and val > 0:
            em(engine, lambda eng, s=s, val=val: eng.wait_ge(sem[s], val))

    def WT(engine, tok):
        if tok is not None:
            W(engine, tok[0], tok[1])

    def dma(engine, stream, build):
        v = iv(stream, 16)
        em(engine, lambda eng, build=build, stream=stream: build(eng).then_inc(sem[stream], 16))
        return (stream, v)

    e = top.enter_context
    ident_sb = e(nc.sbuf_tensor("ident_sb", [128, 128], BF16))
    csb = e(nc.sbuf_tensor("csb", [128, 2], F32))
    gacc = e(nc.sbuf_tensor("gacc", [128, 144], F32))
    gsum = e(nc.sbuf_tensor("gsum", [128, 4], F32))
    gbc = e(nc.sbuf_tensor("gbc", [128, 128], F32))
    cbc = e(nc.sbuf_tensor("cbc", [128, 4], F32))
    ncb = e(nc.sbuf_tensor("ncb", [128, 4], F32))
    vstat = e(nc.sbuf_tensor("vstat", [128, 8], F32))
    astat = e(nc.sbuf_tensor("astat", [128, 8], F32))

    tok_ident = dma("sync", "dId", lambda eng: eng.dma_start(ident_sb[:, :], ident[:, :]))
    tok_csb = dma("sync", "dCs", lambda eng: eng.dma_start(csb[:, :], consts[:, :]))
    tok_outinit = dma("gpsimd", "dO", lambda eng: eng.dma_start(out[:, :], xb[0:T, :]))

    ab = open_stack()
    xnT_sb = ab.enter_context(nc.sbuf_tensor("xnT_sb", [128, DT * T], BF16, side="right"))

    phRQ = open_stack()
    ring = phRQ.enter_context(nc.sbuf_tensor("ring", [128, 2 * D], F32, side="right"))
    phA = open_stack()
    ea = phA.enter_context
    xn_sb = ea(nc.sbuf_tensor("xn_sb", [128, NXT * D], BF16, side="left"))
    phA1 = open_stack()
    ea1 = phA1.enter_context
    scratch = ea1(nc.sbuf_tensor("scratch", [128, D], BF16, side="left"))
    att_sb = ea1(nc.sbuf_tensor("att_sb", [128, D], BF16, side="left"))
    attTg = ea1(nc.sbuf_tensor("attTg", [128, DT * 128], BF16, side="left"))
    stage = ea1(nc.sbuf_tensor("stage", [128, 2 * 128], F32, side="left"))
    ps_g = ea(nc.psum_tensor("ps_g", [128, 2048], F32))
    ps_tp = [ea(nc.psum_tensor("ps_tp0", [128, 512], F32)),
             ea(nc.psum_tensor("ps_tp1", [128, 512], F32))]
    ps_o = [ea(nc.psum_tensor("ps_o0", [128, 512], F32)),
            ea(nc.psum_tensor("ps_o1", [128, 512], F32))]

    rsem = ["dR0", "dR1"]

    v_xn = [0] * NXT
    for i in range(NXT):
        sl = (i % 2) * D
        if i >= 2:
            W("sync", "v", v_xn[i - 2])
        tok = dma("sync", rsem[i % 2], lambda eng, i=i, sl=sl: eng.dma_start(
            ring[:, sl:sl + D], xb[i * 128:(i + 1) * 128, :]))

        WT("vector", tok)
        em("vector", lambda eng, sl=sl: eng.reduce_sum(
            vstat[:, 0:1], ring[:, sl:sl + D], AX.X))
        em("vector", lambda eng, sl=sl: eng.scalar_tensor_tensor(
            scratch[:, :], ring[:, sl:sl + D], 1.0, ring[:, sl:sl + D],
            ALU.mult, ALU.mult, accum_out=vstat[:, 1:2]).then_inc(sem["v"], 1))
        W("vector", "v", iv("v"))
        em("vector", lambda eng: eng.tensor_tensor(
            vstat[:, 3:4], vstat[:, 0:1], vstat[:, 0:1], ALU.mult))
        em("vector", lambda eng: eng.tensor_scalar_mul(
            vstat[:, 2:3], vstat[:, 0:1], 1.0 / D).then_inc(sem["v"], 1))
        W("vector", "v", iv("v"))
        em("vector", lambda eng: eng.scalar_tensor_tensor(
            vstat[:, 4:5], vstat[:, 1:2], float(D), vstat[:, 3:4],
            ALU.mult, ALU.subtract).then_inc(sem["v"], 1))
        h3 = iv("v")

        if i == 0:
            WT("scalar", tok_csb)
        if i > 0:
            W("scalar", "v", v_xn[i - 1])
        W("scalar", "v", h3)
        em("scalar", lambda eng: eng.activation(
            astat[:, 0:1], vstat[:, 4:5], AF.Ln, bias=csb[:, 0:1]).then_inc(sem["a"], 1))
        W("scalar", "a", iv("a"))
        em("scalar", lambda eng: eng.activation(
            astat[:, 1:2], astat[:, 0:1], AF.Exp, scale=-0.5, bias=csb[:, 1:2]).then_inc(sem["a"], 1))
        a_rstd = iv("a")

        W("vector", "a", a_rstd)
        em("vector", lambda eng, i=i, sl=sl: eng.tensor_scalar(
            xn_sb[:, i * D:(i + 1) * D], ring[:, sl:sl + D],
            vstat[:, 2:3], astat[:, 1:2], ALU.subtract, ALU.mult).then_inc(sem["v"], 1))
        v_xn[i] = iv("v")

    tp_evacs = []
    tp_count = 0
    WT("tensor", tok_ident)
    for tt in range(NLT):
        W("tensor", "v", v_xn[tt])
        for dt in range(DT):
            pp = tp_count % 2
            if tp_count >= 2:
                W("tensor", "a", tp_evacs[tp_count - 2])
            em("tensor", lambda eng, tt=tt, dt=dt, pp=pp: eng.matmul(
                ps_tp[pp][:, 0:128],
                xn_sb[:, tt * D + dt * 128: tt * D + dt * 128 + 128],
                ident_sb[:, :], start=True, stop=True).then_inc(sem["t"], 1))
            tv = iv("t")
            W("scalar", "t", tv)
            em("scalar", lambda eng, tt=tt, dt=dt, pp=pp: eng.copy(
                xnT_sb[:, dt * T + tt * 128: dt * T + tt * 128 + 128],
                ps_tp[pp][:, 0:128]).then_inc(sem["a"], 1))
            tp_evacs.append(iv("a"))
            tp_count += 1

    a_exp = [0] * IT
    v_att = [0] * IT
    t_attT_last = [0] * IT
    t_ao_last = [0] * NG
    v_aoe = {}
    tok_ao = {}
    ao_seq = []
    asem = ["dA0", "dA1"]
    W("tensor", "v", v_xn[NXT - 1])
    for g in range(NG):
        it = g
        if it > 0:
            W("tensor", "a", a_exp[it - 1])
        for ct in range(NXT):
            for c4 in range(4):
                st, sp = (ct == 0), (ct == NXT - 1)
                def mmg(eng, it=it, ct=ct, c4=c4, st=st, sp=sp):
                    return eng.matmul(
                        ps_g[:, c4 * 512:(c4 + 1) * 512],
                        xn_sb[:, ct * D + it * 128: ct * D + it * 128 + 128],
                        xn_sb[:, ct * D + c4 * 512: ct * D + (c4 + 1) * 512],
                        start=st, stop=sp)
                if sp and c4 == 3:
                    em("tensor", lambda eng, mmg=mmg: mmg(eng).then_inc(sem["t"], 1))
                else:
                    em("tensor", mmg)
        t_gram = iv("t")

        W("vector", "t", t_gram)
        if it >= 1:
            W("vector", "t", t_attT_last[it - 1])
        em("vector", lambda eng: eng.reduce_max(
            vstat[:, 4:5], ps_g[:, :], AX.X, negate=True).then_inc(sem["v"], 1))
        v_max = iv("v")
        W("scalar", "v", v_max)
        if it > 0:
            W("scalar", "v", v_att[it - 1])
        em("scalar", lambda eng: eng.activation(
            scratch[:, :], ps_g[:, :], AF.Exp,
            bias=vstat[:, 4:5], accum_out=astat[:, 2:3]).then_inc(sem["a"], 1))
        a_exp[it] = iv("a")
        W("vector", "a", a_exp[it])
        em("vector", lambda eng: eng.reciprocal(
            vstat[:, 5:6], astat[:, 2:3]).then_inc(sem["v"], 1))
        W("vector", "v", iv("v"))
        em("vector", lambda eng: eng.tensor_scalar_mul(
            att_sb[:, 0:D], scratch[:, :], vstat[:, 5:6]).then_inc(sem["v"], 1))
        v_att[it] = iv("v")

        W("tensor", "v", v_att[it])
        for jt in range(DT):
            pp = tp_count % 2
            W("tensor", "a", tp_evacs[tp_count - 2])
            em("tensor", lambda eng, jt=jt, pp=pp: eng.matmul(
                ps_tp[pp][:, 0:128],
                att_sb[:, jt * 128: jt * 128 + 128],
                ident_sb[:, :], start=True, stop=True).then_inc(sem["t"], 1))
            tv = iv("t")
            if g > 0 and jt == 0:
                W("scalar", "t", t_ao_last[g - 1])
            W("scalar", "t", tv)
            em("scalar", lambda eng, jt=jt, pp=pp: eng.copy(
                attTg[:, jt * 128:(jt + 1) * 128],
                ps_tp[pp][:, 0:128]).then_inc(sem["a"], 1))
            tp_evacs.append(iv("a"))
            tp_count += 1
        t_attT_last[it] = VAL["t"]

        W("tensor", "a", tp_evacs[-1])
        for tt in range(NLT):
            pp = tt % 2
            key = (g, tt)
            if len(ao_seq) >= 2:
                W("tensor", "v", v_aoe[ao_seq[-2]])
            for jt in range(DT):
                st, sp = (jt == 0), (jt == DT - 1)
                def mmo(eng, tt=tt, jt=jt, pp=pp, st=st, sp=sp):
                    return eng.matmul(
                        ps_o[pp][:, 0:128],
                        xnT_sb[:, jt * T + tt * 128: jt * T + tt * 128 + 128],
                        attTg[:, jt * 128:(jt + 1) * 128],
                        start=st, stop=sp)
                if sp:
                    em("tensor", lambda eng, mmo=mmo: mmo(eng).then_inc(sem["t"], 1))
                else:
                    em("tensor", mmo)
            t_ao = iv("t")
            W("vector", "t", t_ao)
            if len(ao_seq) >= 2:
                WT("vector", tok_ao[ao_seq[-2]])
            em("vector", lambda eng, pp=pp: eng.tensor_copy(
                stage[:, pp * 128:(pp + 1) * 128], ps_o[pp][:, 0:128]).then_inc(sem["v"], 1))
            v_aoe[key] = iv("v")
            W("gpsimd", "v", v_aoe[key])
            if not ao_seq:
                WT("gpsimd", tok_outinit)
            tok_ao[key] = dma("gpsimd", asem[pp], lambda eng, g=g, tt=tt, pp=pp: eng.dma_start(
                out[tt * 128:(tt + 1) * 128, g * 128:(g + 1) * 128],
                stage[:, pp * 128:(pp + 1) * 128],
                accum_op=ALU.add))
            ao_seq.append(key)
        t_ao_last[g] = VAL["t"]

    tile_specs = []
    wcol = 0
    for m, wd in ((0, w1), (1, w2)):
        for j in range(HT):
            tile_specs.append((wd, j * 128, 0, D, m, wcol)); wcol += 1
    for j in range(DT):
        for c in range(3):
            c0 = c * 2048
            wdt = 2048 if c < 2 else H - 4096
            tile_specs.append((w3, j * 128, c0, wdt, 2, wcol)); wcol += 1
    assert wcol <= 144
    col_rng = [(0, 44), (44, 88), (88, 136)]

    v_gred = []
    for idx, (wd, r0, c0, wdt, m, gc) in enumerate(tile_specs):
        sl = (idx % 2) * D
        W("sync", "v", v_xn[NXT - 1] if idx < 2 else v_gred[idx - 2])
        tok = dma("sync", rsem[idx % 2], lambda eng, wd=wd, r0=r0, c0=c0, wdt=wdt, sl=sl:
                  eng.dma_start(ring[:, sl:sl + wdt], wd[r0:r0 + 128, c0:c0 + wdt]))
        WT("vector", tok)
        em("vector", lambda eng, sl=sl, wdt=wdt, gc=gc: eng.tensor_reduce(
            gacc[:, gc:gc + 1], ring[:, sl:sl + wdt], AX.X, ALU.add,
            apply_absolute_value=True).then_inc(sem["v"], 1))
        v_gred.append(iv("v"))

    v_cready = [0] * 3
    for m in range(3):
        lo, hi = col_rng[m]
        W("vector", "v", v_gred[hi - 1])
        em("vector", lambda eng, lo=lo, hi=hi, m=m: eng.tensor_reduce(
            gsum[:, m:m + 1], gacc[:, lo:hi], AX.X, ALU.add).then_inc(sem["v"], 1))
        v_gs = iv("v")
        W("gpsimd", "v", v_gs)
        t1 = dma("gpsimd", "dGF", lambda eng, m=m: eng.dma_start(
            gscr[m:m + 1, :], gsum[:, m:m + 1], allow_slow_non_contiguous=True))
        WT("gpsimd", t1)
        t2 = dma("gpsimd", "dGF", lambda eng, m=m: eng.dma_start(
            gbc[:, :], bass.AP(gscr, m * 128, [[0, 128], [1, 128]])))
        WT("vector", t2)
        em("vector", lambda eng: eng.reduce_sum(
            vstat[:, 6:7], gbc[:, :], AX.X).then_inc(sem["v"], 1))
        W("vector", "v", iv("v"))
        em("vector", lambda eng, m=m: eng.tensor_scalar(
            cbc[:, m:m + 1], vstat[:, 6:7], 0.5 / NELEM, 0.5e-8,
            ALU.mult, ALU.add).then_inc(sem["v"], 1))
        W("vector", "v", iv("v"))
        em("vector", lambda eng, m=m: eng.tensor_scalar_mul(
            ncb[:, m:m + 1], cbc[:, m:m + 1], -1.0).then_inc(sem["v"], 1))
        v_cready[m] = iv("v")

    phA1.close()
    lt_sb = phRQ.enter_context(nc.sbuf_tensor("lt_sb", [128, D], BF16, side="right"))
    qt_sb = phRQ.enter_context(nc.sbuf_tensor("qt_sb", [128, 2 * D], BF16, side="right"))
    W("vector", "t", t_ao_last[NG - 1])
    W("vector", "a", tp_evacs[-1])
    WT("vector", tok_ao[(NG - 1, NLT - 1)])
    WT("vector", tok_ao[(NG - 1, NLT - 2)])
    qsem = ["dQ0", "dQ1"]
    v_q = []
    tok_q = []
    qw_snap = [[None] * HT, [None] * HT, [None] * DT]
    for idx, (wd, r0, c0, wdt, m, gc) in enumerate(tile_specs):
        sl = (idx % 2) * D
        W("sync", "v", v_gred[-1] if idx < 2 else v_q[idx - 2])
        tok = dma("sync", rsem[idx % 2], lambda eng, wd=wd, r0=r0, c0=c0, wdt=wdt, sl=sl:
                  eng.dma_start(ring[:, sl:sl + wdt], wd[r0:r0 + 128, c0:c0 + wdt]))
        WT("vector", tok)
        W("vector", "v", v_cready[m])
        qp = idx % 2
        if idx >= 2:
            WT("vector", tok_q[idx - 2])
        if idx >= 1:
            W("vector", "v", v_q[idx - 1])
        em("vector", lambda eng, sl=sl, wdt=wdt, m=m: eng.tensor_single_scalar(
            lt_sb[:, 0:wdt], ring[:, sl:sl + wdt], ncb[:, m:m + 1], ALU.is_lt).then_inc(sem["v"], 1))
        W("vector", "v", iv("v"))
        em("vector", lambda eng, sl=sl, wdt=wdt, m=m, qp=qp: eng.scalar_tensor_tensor(
            qt_sb[:, qp * D: qp * D + wdt], ring[:, sl:sl + wdt], cbc[:, m:m + 1],
            lt_sb[:, 0:wdt], ALU.is_gt, ALU.subtract).then_inc(sem["v"], 1))
        v_done = iv("v")
        v_q.append(v_done)
        W("gpsimd", "v", v_done)
        tk = dma("gpsimd", qsem[qp], lambda eng, qd=qs[m], r0=r0, c0=c0, wdt=wdt, qp=qp:
                 eng.dma_start(qd[r0:r0 + 128, c0:c0 + wdt], qt_sb[:, qp * D: qp * D + wdt]))
        tok_q.append(tk)
        qw_snap[m][r0 // 128] = (VAL["dQ0"], VAL["dQ1"])

    barA_t = VAL["t"]
    barA_a = tp_evacs[-1]
    barA_vaoe = v_aoe[(NG - 1, NLT - 1)]
    for engn in ("sync", "tensor", "scalar"):
        W(engn, "t", barA_t)
        W(engn, "a", barA_a)
        W(engn, "v", barA_vaoe)
        WT(engn, tok_ao[(NG - 1, NLT - 1)])
        WT(engn, tok_ao[(NG - 1, NLT - 2)])
    phA.close()

    b0 = open_stack()
    hT_sb = b0.enter_context(nc.sbuf_tensor("hT_sb", [128, HT * T], BF16, side="left"))
    phB = open_stack()
    ebx = phB.enter_context
    w1T_sb = ebx(nc.sbuf_tensor("w1T_sb", [128, 2 * DT * WH], BF16, side="left"))
    w2T_sb = ebx(nc.sbuf_tensor("w2T_sb", [128, 2 * DT * WH], BF16, side="left"))
    ps_x = [ebx(nc.psum_tensor("ps_xa", [128, 1024], F32)),
            ebx(nc.psum_tensor("ps_xb", [128, 1024], F32))]

    ffn_state = {"unit": 0, "ev": [], "sig_ev": [0] * HT}

    def ffn12(wT_sb, qd, qw_m, kind, wsems):
        t_cons = [0] * NW
        for hw in range(NW):
            s0, s1 = qw_m[2 * hw + 1]
            W("sync", "dQ0", s0)
            W("sync", "dQ1", s1)
            if hw >= 2:
                W("sync", "t", t_cons[hw - 2])
            buf = (hw % 2) * DT * WH
            wsem = wsems[hw % 2]
            last_tok = None
            for K in range(DT):
                last_tok = dma("sync", wsem, lambda eng, qd=qd, hw=hw, K=K, buf=buf:
                               eng.dma_start(
                                   wT_sb[:, buf + K * WH: buf + (K + 1) * WH],
                                   qd[hw * WH:(hw + 1) * WH, K * 128:(K + 1) * 128],
                                   transpose=True))
            WT("tensor", last_tok)
            for mt in range(2):
                slab = hw * 2 + mt
                unit = ffn_state["unit"]
                pp = unit % 2
                if unit >= 2:
                    sname, sval = ffn_state["ev"][unit - 2]
                    W("tensor", sname, sval)
                for K in range(DT):
                    for c in range(2):
                        def mmf(eng, buf=buf, K=K, mt=mt, c=c, pp=pp):
                            return eng.matmul(
                                ps_x[pp][:, c * 512:(c + 1) * 512],
                                wT_sb[:, buf + K * WH + mt * 128: buf + K * WH + mt * 128 + 128],
                                xnT_sb[:, K * T + c * 512: K * T + (c + 1) * 512],
                                start=(K == 0), stop=(K == DT - 1))
                        if K == DT - 1 and c == 1:
                            em("tensor", lambda eng, mmf=mmf: mmf(eng).then_inc(sem["t"], 1))
                        else:
                            em("tensor", mmf)
                tmm = iv("t")
                if mt == 1:
                    t_cons[hw] = tmm
                if kind == "sig":
                    W("scalar", "t", tmm)
                    em("scalar", lambda eng, slab=slab, pp=pp: eng.activation(
                        hT_sb[:, slab * T:(slab + 1) * T], ps_x[pp][:, :],
                        AF.Sigmoid).then_inc(sem["a"], 1))
                    av = iv("a")
                    ffn_state["ev"].append(("a", av))
                    ffn_state["sig_ev"][slab] = av
                else:
                    W("vector", "t", tmm)
                    W("vector", "a", ffn_state["sig_ev"][slab])
                    em("vector", lambda eng, slab=slab, pp=pp: eng.tensor_tensor(
                        hT_sb[:, slab * T:(slab + 1) * T],
                        hT_sb[:, slab * T:(slab + 1) * T],
                        ps_x[pp][:, :], ALU.mult).then_inc(sem["v"], 1))
                    ffn_state["ev"].append(("v", iv("v")))
                ffn_state["unit"] += 1

    ffn12(w1T_sb, q1, qw_snap[0], "sig", ("dWa", "dWb"))
    ffn12(w2T_sb, q2, qw_snap[1], "mul", ("dWc", "dWd"))

    barB = dict(VAL)
    for engn in ("sync", "vector", "scalar", "tensor", "gpsimd"):
        for s in SEMS:
            W(engn, s, barB[s])
    phB.close()
    phRQ.close()
    ab.close()

    phC = open_stack()
    ec = phC.enter_context
    w3T_sb = ec(nc.sbuf_tensor("w3T_sb", [128, 2 * HT * 512], BF16))
    fstage = ec(nc.sbuf_tensor("fstage", [128, 2 * 512], F32))
    ps_f = [ec(nc.psum_tensor("ps_f0", [128, 512], F32)),
            ec(nc.psum_tensor("ps_f1", [128, 512], F32))]

    fsem = ["dF0", "dF1"]
    wsem3 = ["dWe", "dWf"]
    t3_cons = [0] * DBN
    f_seq = []
    v_fe = {}
    tok_f = {}
    for db in range(DBN):
        s0, s1 = qw_snap[2][4 * db + 3]
        W("sync", "dQ0", s0)
        W("sync", "dQ1", s1)
        if db >= 2:
            W("sync", "t", t3_cons[db - 2])
        buf = (db % 2) * HT * 512
        last_tok = None
        for K in range(HT):
            last_tok = dma("sync", wsem3[db % 2], lambda eng, db=db, K=K, buf=buf:
                           eng.dma_start(
                               w3T_sb[:, buf + K * 512: buf + (K + 1) * 512],
                               q3[db * 512:(db + 1) * 512, K * 128:(K + 1) * 128],
                               transpose=True))
        WT("tensor", last_tok)
        for mt in range(NLT):
            pp = len(f_seq) % 2
            key = (db, mt)
            if len(f_seq) >= 2:
                W("tensor", "v", v_fe[f_seq[-2]])
            for K in range(HT):
                def mm3(eng, buf=buf, K=K, mt=mt, pp=pp):
                    return eng.matmul(
                        ps_f[pp][:, 0:512],
                        hT_sb[:, K * T + mt * 128: K * T + mt * 128 + 128],
                        w3T_sb[:, buf + K * 512: buf + (K + 1) * 512],
                        start=(K == 0), stop=(K == HT - 1))
                if K == HT - 1:
                    em("tensor", lambda eng, mm3=mm3: mm3(eng).then_inc(sem["t"], 1))
                else:
                    em("tensor", mm3)
            t_f = iv("t")
            if mt == NLT - 1:
                t3_cons[db] = t_f
            W("vector", "t", t_f)
            if len(f_seq) >= 2:
                WT("vector", tok_f[f_seq[-2]])
            em("vector", lambda eng, pp=pp: eng.tensor_copy(
                fstage[:, pp * 512:(pp + 1) * 512], ps_f[pp][:, 0:512]).then_inc(sem["v"], 1))
            v_fe[key] = iv("v")
            W("gpsimd", "v", v_fe[key])
            tok_f[key] = dma("gpsimd", fsem[pp], lambda eng, db=db, mt=mt, pp=pp:
                             eng.dma_start(
                                 out[mt * 128:(mt + 1) * 128, db * 512:(db + 1) * 512],
                                 fstage[:, pp * 512:(pp + 1) * 512],
                                 accum_op=ALU.add))
            f_seq.append(key)

    for s in DMA_SEMS:
        W("sync", s, VAL[s])
        W("gpsimd", s, VAL[s])

    with nc.Block() as block:
        @block.sync
        def _(eng):
            for f in streams["sync"]:
                f(eng)

        @block.vector
        def _(eng):
            for f in streams["vector"]:
                f(eng)

        @block.scalar
        def _(eng):
            for f in streams["scalar"]:
                f(eng)

        @block.tensor
        def _(eng):
            for f in streams["tensor"]:
                f(eng)

        @block.gpsimd
        def _(eng):
            for f in streams["gpsimd"]:
                f(eng)

    phC.close()
    b0.close()
    top.close()
    return nc


_EYE = np.eye(128, dtype=ml_dtypes.bfloat16)
_CONSTS = np.concatenate([
    np.full((128, 1), GBIAS, np.float32),
    np.full((128, 1), LNN, np.float32)], axis=1)


def make_in_maps(x, w1, w2, w3):
    in_maps = []
    for k in range(8):
        b, t0 = k // 4, (k % 4) * 1024
        xbk = np.concatenate([x[b, t0:t0 + 1024], x[b, :t0], x[b, t0 + 1024:]], axis=0)
        in_maps.append({
            "xb": np.ascontiguousarray(xbk, dtype=np.float32),
            "w1": np.ascontiguousarray(w1, dtype=np.float32),
            "w2": np.ascontiguousarray(w2, dtype=np.float32),
            "w3": np.ascontiguousarray(w3, dtype=np.float32),
            "ident": _EYE,
            "consts": _CONSTS,
        })
    return in_maps


def kernel(L0_attn_x, L0_ffn_w1, L0_ffn_w2, L0_ffn_w3):
    x = np.asarray(L0_attn_x, dtype=np.float32)
    w1 = np.asarray(L0_ffn_w1, dtype=np.float32)
    w2 = np.asarray(L0_ffn_w2, dtype=np.float32)
    w3 = np.asarray(L0_ffn_w3, dtype=np.float32)
    nc = build_nc()
    res = run_bass_kernel_spmd(nc, make_in_maps(x, w1, w2, w3), core_ids=list(range(8)))
    outp = np.empty((2, 4096, 2048), np.float32)
    for k, r in enumerate(res.results):
        outp[k // 4, (k % 4) * 1024:((k % 4) + 1) * 1024] = r["out"]
    return outp


# version 06
# speedup vs baseline: 1.1587x; 1.1587x over previous
import math
import numpy as np
import ml_dtypes
from contextlib import ExitStack

import concourse.bass as bass
import concourse.mybir as mybir
from concourse.bass_utils import run_bass_kernel_spmd

F32 = mybir.dt.float32
BF16 = mybir.dt.bfloat16
U32 = mybir.dt.uint32
AF = mybir.ActivationFunctionType
ALU = mybir.AluOpType
AX = mybir.AxisListType

S, D, H, T = 4096, 2048, 5632, 1024
NLT = T // 128
NXT = S // 128
DT = D // 128
HT = H // 128
NIT = 4
HSL = H // 4
W3R = D // 8
NELEM = H * D
GBIAS = float(D) * float(D) * 1e-5
LNN = math.log(float(D))
W3C = [(0, 2048), (2048, 2048), (4096, 1536)]
SPANS12 = [(0, 1024), (1024, 1024), (2048, 1024), (3072, 1024), (4096, 1024), (5120, 512)]
SPANS3 = [0, 512, 1024, 1536]

SEMS = ("v", "a", "t", "c",
        "dId", "dCs", "dMi", "dO", "dGF", "dBn", "dB0", "dB1", "dXI",
        "dX0", "dX1", "dN0", "dN1", "dAT0", "dAT1",
        "dQl0", "dQl1", "dQw0", "dQw1", "dG0", "dG1", "dQ2", "dQ2b",
        "dWa", "dWb", "dWc", "dWd", "dA0", "dA1", "dF0", "dF1")
DMA_SEMS = tuple(s for s in SEMS if s.startswith("d"))


def build_nc():
    nc = bass.Bass("TRN2", target_bir_lowering=False)

    xloc = nc.dram_tensor("xloc", [T, D], F32, kind="ExternalInput")
    wsl = nc.dram_tensor("wsl", [HSL, D], F32, kind="ExternalInput")
    w3sl = nc.dram_tensor("w3sl", [W3R, H], F32, kind="ExternalInput")
    w1 = nc.dram_tensor("w1", [H, D], F32, kind="ExternalInput")
    w2 = nc.dram_tensor("w2", [H, D], F32, kind="ExternalInput")
    w3 = nc.dram_tensor("w3", [D, H], F32, kind="ExternalInput")
    ident = nc.dram_tensor("ident", [128, 128], BF16, kind="ExternalInput")
    consts = nc.dram_tensor("consts", [128, 2], F32, kind="ExternalInput")
    ioff = nc.dram_tensor("ioff", [1, 1], U32, kind="ExternalInput")
    grpflag = nc.dram_tensor("grpflag", [1, 2], F32, kind="ExternalInput")
    out = nc.dram_tensor("out", [T, D], F32, kind="ExternalOutput")

    q1 = nc.dram_tensor("q1", [H, D], BF16, kind="Internal")
    q2 = nc.dram_tensor("q2", [H, D], BF16, kind="Internal")
    q3 = nc.dram_tensor("q3", [D, H], BF16, kind="Internal")
    cc_xn_in = nc.dram_tensor("cc_xn_in", [T, D], BF16, kind="Internal")
    cc_xn_out = nc.dram_tensor("cc_xn_out", [S, D], BF16, kind="Internal")
    cc_att_in = nc.dram_tensor("cc_att_in", [1, D * 512], BF16, kind="Internal")
    cc_att_out = nc.dram_tensor("cc_att_out", [4, D * 512], BF16, kind="Internal")
    cc_g_in = nc.dram_tensor("cc_g_in", [1, 2], F32, kind="Internal")
    cc_g_out = nc.dram_tensor("cc_g_out", [8, 2], F32, kind="Internal")
    gscr = nc.dram_tensor("gscr", [2, 128], F32, kind="Internal")
    cscr = nc.dram_tensor("cscr", [1, 4], F32, kind="Internal")

    streams = {e: [] for e in ("sync", "vector", "scalar", "tensor", "gpsimd")}
    VAL = {s: 0 for s in SEMS}

    def em(engine, fn):
        streams[engine].append(fn)

    def iv(s, amt=1):
        VAL[s] += amt
        return VAL[s]

    stacks = []

    def open_stack():
        st = ExitStack()
        stacks.append(st)
        return st

    top = open_stack()
    sem = {n: top.enter_context(nc.semaphore(n)) for n in SEMS}

    def W(engine, s, val):
        if val and val > 0:
            em(engine, lambda eng, s=s, val=val: eng.wait_ge(sem[s], val))

    def WT(engine, tok):
        if tok is not None:
            W(engine, tok[0], tok[1])

    def dma(engine, stream, build):
        v = iv(stream, 16)
        em(engine, lambda eng, build=build, stream=stream: build(eng).then_inc(sem[stream], 16))
        return (stream, v)

    e = top.enter_context
    ident_sb = e(nc.sbuf_tensor("ident_sb", [128, 128], BF16))
    csb = e(nc.sbuf_tensor("csb", [128, 2], F32))
    ioffs = e(nc.sbuf_tensor("ioffs", [1, 2], U32))
    gfs = e(nc.sbuf_tensor("gfs", [1, 2], F32))
    gacc = e(nc.sbuf_tensor("gacc", [128, 24], F32))
    gsum = e(nc.sbuf_tensor("gsum", [128, 2], F32))
    gbc = e(nc.sbuf_tensor("gbc", [128, 256], F32))
    gag = e(nc.sbuf_tensor("gag", [1, 16], F32))
    gtmp = e(nc.sbuf_tensor("gtmp", [1, 8], F32))
    cb2 = e(nc.sbuf_tensor("cb2", [128, 4], F32))
    ncb2 = e(nc.sbuf_tensor("ncb2", [128, 4], F32))
    vstat = e(nc.sbuf_tensor("vstat", [128, 8], F32))
    astat = e(nc.sbuf_tensor("astat", [128, 8], F32))
    ps_all = e(nc.psum_tensor("ps_all", [128, 4096], F32))

    tok_ident = dma("sync", "dId", lambda eng: eng.dma_start(ident_sb[:, :], ident[:, :]))
    tok_csb = dma("sync", "dCs", lambda eng: eng.dma_start(csb[:, :], consts[:, :]))
    dma("sync", "dMi", lambda eng: eng.dma_start(ioffs[0:1, 0:1], ioff[0:1, 0:1]))
    dma("sync", "dMi", lambda eng: eng.dma_start(gfs[:, :], grpflag[:, :]))
    tok_outinit = dma("gpsimd", "dO", lambda eng: eng.dma_start(out[:, :], xloc[:, :]))

    abx = open_stack()
    xnT_sb = abx.enter_context(nc.sbuf_tensor("xnT_sb", [128, DT * T], BF16, side="right"))
    phA = open_stack()
    ea = phA.enter_context
    ring = ea(nc.sbuf_tensor("ring", [128, 2 * D], F32, side="right"))
    ring2 = ea(nc.sbuf_tensor("ring2", [128, 2 * D], F32, side="right"))
    xn_loc = ea(nc.sbuf_tensor("xn_loc", [128, NLT * D], BF16, side="left"))
    nring = ea(nc.sbuf_tensor("nring", [128, 2 * D], BF16, side="left"))
    xn_i = ea(nc.sbuf_tensor("xn_i", [128, NXT * 512], BF16, side="left"))
    scratch = ea(nc.sbuf_tensor("scratch", [128, D], BF16, side="left"))
    att_sb = ea(nc.sbuf_tensor("att_sb", [128, D], BF16, side="left"))
    attTit = ea(nc.sbuf_tensor("attTit", [128, DT * 128], BF16, side="left"))
    attTf = ea(nc.sbuf_tensor("attTf", [128, DT * 512], BF16, side="left"))
    stage = ea(nc.sbuf_tensor("stage", [128, 2 * 512], F32, side="left"))
    lt_sb = ea(nc.sbuf_tensor("lt_sb", [128, D], BF16, side="left"))
    qt_sb = ea(nc.sbuf_tensor("qt_sb", [128, 2 * D], BF16, side="left"))
    s12_sb = ea(nc.sbuf_tensor("s12_sb", [128, 2 * D], BF16, side="left"))
    qt2_sb = ea(nc.sbuf_tensor("qt2_sb", [128, 2 * D], BF16, side="left"))

    rsem = ["dX0", "dX1"]
    nsem = ["dN0", "dN1"]

    v_xn = [0] * NLT
    for i in range(NLT):
        sl = (i % 2) * D
        if i >= 2:
            W("sync", "v", v_xn[i - 2])
        tok = dma("sync", rsem[i % 2], lambda eng, i=i, sl=sl: eng.dma_start(
            ring[:, sl:sl + D], xloc[i * 128:(i + 1) * 128, :]))
        WT("vector", tok)
        em("vector", lambda eng, sl=sl: eng.reduce_sum(
            vstat[:, 0:1], ring[:, sl:sl + D], AX.X))
        em("vector", lambda eng, sl=sl: eng.scalar_tensor_tensor(
            scratch[:, :], ring[:, sl:sl + D], 1.0, ring[:, sl:sl + D],
            ALU.mult, ALU.mult, accum_out=vstat[:, 1:2]).then_inc(sem["v"], 1))
        W("vector", "v", iv("v"))
        em("vector", lambda eng: eng.tensor_tensor(
            vstat[:, 3:4], vstat[:, 0:1], vstat[:, 0:1], ALU.mult))
        em("vector", lambda eng: eng.tensor_scalar_mul(
            vstat[:, 2:3], vstat[:, 0:1], 1.0 / D).then_inc(sem["v"], 1))
        W("vector", "v", iv("v"))
        em("vector", lambda eng: eng.scalar_tensor_tensor(
            vstat[:, 4:5], vstat[:, 1:2], float(D), vstat[:, 3:4],
            ALU.mult, ALU.subtract).then_inc(sem["v"], 1))
        h3 = iv("v")
        if i == 0:
            WT("scalar", tok_csb)
        if i > 0:
            W("scalar", "v", v_xn[i - 1])
        W("scalar", "v", h3)
        em("scalar", lambda eng: eng.activation(
            astat[:, 0:1], vstat[:, 4:5], AF.Ln, bias=csb[:, 0:1]).then_inc(sem["a"], 1))
        W("scalar", "a", iv("a"))
        em("scalar", lambda eng: eng.activation(
            astat[:, 1:2], astat[:, 0:1], AF.Exp, scale=-0.5, bias=csb[:, 1:2]).then_inc(sem["a"], 1))
        a_rstd = iv("a")
        W("vector", "a", a_rstd)
        em("vector", lambda eng, i=i, sl=sl: eng.tensor_scalar(
            xn_loc[:, i * D:(i + 1) * D], ring[:, sl:sl + D],
            vstat[:, 2:3], astat[:, 1:2], ALU.subtract, ALU.mult).then_inc(sem["v"], 1))
        v_xn[i] = iv("v")

    W("gpsimd", "v", v_xn[NLT - 1])
    tok_xnw = dma("gpsimd", "dBn", lambda eng: eng.dma_start(
        bass.AP(cc_xn_in, 0, [[D, 128], [128 * D, NLT], [1, D]]),
        bass.AP(xn_loc, 0, [[NLT * D, 128], [D, NLT], [1, D]])))
    WT("gpsimd", tok_xnw)
    em("gpsimd", lambda eng: eng.collective_compute(
        "AllGather", ALU.bypass,
        replica_groups=[[0, 1, 2, 3], [4, 5, 6, 7]],
        ins=[cc_xn_in.ap().opt()], outs=[cc_xn_out.ap().opt()]).then_inc(sem["c"]))
    c_xn = iv("c")

    W("sync", "dMi", 32)
    W("sync", "c", c_xn)

    def load_xni(eng):
        with eng.register("rio") as rio:
            eng.reg_load(rio, ioffs[0:1, 0:1])
            off = eng.snap(rio)
            src = cc_xn_out.ap().rearrange("(s p) c -> p s c", p=128)
            return eng.dma_start(
                bass.AP(xn_i, 0, [[NXT * 512, 128], [512, NXT], [1, 512]]),
                src[:, :, bass.ds(off, 512)])
    tok_xni = dma("sync", "dXI", load_xni)

    ps_tp = [ps_all[:, 2048:2560], ps_all[:, 2560:3072]]
    tp_evacs = []
    tp_count = 0
    WT("tensor", tok_ident)
    for tt in range(NLT):
        W("tensor", "v", v_xn[tt])
        for dt in range(DT):
            pp = tp_count % 2
            if tp_count >= 2:
                W("tensor", "a", tp_evacs[tp_count - 2])
            em("tensor", lambda eng, tt=tt, dt=dt, pp=pp: eng.matmul(
                ps_tp[pp][:, 0:128],
                xn_loc[:, tt * D + dt * 128: tt * D + dt * 128 + 128],
                ident_sb[:, :], start=True, stop=True).then_inc(sem["t"], 1))
            tv = iv("t")
            W("scalar", "t", tv)
            em("scalar", lambda eng, tt=tt, dt=dt, pp=pp: eng.copy(
                xnT_sb[:, dt * T + tt * 128: dt * T + tt * 128 + 128],
                ps_tp[pp][:, 0:128]).then_inc(sem["a"], 1))
            tp_evacs.append(iv("a"))
            tp_count += 1

    wtile_specs = []
    for j in range(HSL // 128):
        wtile_specs.append((wsl, j * 128, 0, D, j))
    for j3 in range(2):
        for ci, (c0, wdt) in enumerate(W3C):
            wtile_specs.append((w3sl, j3 * 128, c0, wdt, 11 + 3 * j3 + ci))
    NWT = len(wtile_specs)
    v_gred = [0] * NWT
    gamma_iter = iter(range(NWT))
    gamma_emitted = [0]

    def emit_gamma_load():
        try:
            idx = next(gamma_iter)
        except StopIteration:
            return
        wd, r0, c0, wdt, gc = wtile_specs[idx]
        sl = (idx % 2) * D
        W("sync", "v", v_xn[NLT - 1] if idx < 2 else v_gred[idx - 2])
        tok = dma("sync", rsem[idx % 2], lambda eng, wd=wd, r0=r0, c0=c0, wdt=wdt, sl=sl:
                  eng.dma_start(ring[:, sl:sl + wdt], wd[r0:r0 + 128, c0:c0 + wdt]))
        WT("vector", tok)
        em("vector", lambda eng, sl=sl, wdt=wdt, gc=gc: eng.tensor_reduce(
            gacc[:, gc:gc + 1], ring[:, sl:sl + wdt], AX.X, ALU.add,
            apply_absolute_value=True).then_inc(sem["v"], 1))
        v_gred[idx] = iv("v")
        gamma_emitted[0] += 1

    ps_g = [ps_all[:, 0:2048], ps_all[:, 2048:4096]]
    a_exp = [0] * NIT
    v_att = [0] * NIT
    t_attT_l = [0] * NIT
    att_dBn = [None] * NIT
    t_ring_tok = {}

    def emit_gram_pass(ph):
        its = [2 * ph, 2 * ph + 1]
        W("tensor", "a", tp_evacs[-1])
        if ph > 0:
            W("tensor", "a", a_exp[1])
        WT("tensor", tok_xni)
        for ct in range(NXT):
            sl = (ct % 2) * D
            if (ph, ct - 2) in t_ring_tok:
                W("sync", "t", t_ring_tok[(ph, ct - 2)])
            elif ph > 0 and ct < 2:
                W("sync", "t", t_ring_tok[(0, NXT - 2 + ct)])
            tok = dma("sync", nsem[ct % 2], lambda eng, ct=ct, sl=sl: eng.dma_start(
                nring[:, sl:sl + D], cc_xn_out[ct * 128:(ct + 1) * 128, :]))
            if ct % 2 == 0 and ph == 0:
                emit_gamma_load()
            WT("tensor", tok)
            for n in range(2):
                m = its[n]
                for c4 in range(4):
                    st, sp = (ct == 0), (ct == NXT - 1)
                    last = (n == 1 and c4 == 3)
                    def mmg(eng, m=m, n=n, ct=ct, c4=c4, sl=sl, st=st, sp=sp):
                        return eng.matmul(
                            ps_g[n][:, c4 * 512:(c4 + 1) * 512],
                            xn_i[:, ct * 512 + m * 128: ct * 512 + m * 128 + 128],
                            nring[:, sl + c4 * 512: sl + (c4 + 1) * 512],
                            start=st, stop=sp)
                    if last:
                        em("tensor", lambda eng, mmg=mmg: mmg(eng).then_inc(sem["t"], 1))
                    else:
                        em("tensor", mmg)
            t_ring_tok[(ph, ct)] = iv("t")

    def emit_softmax_attT(ph, n):
        it = 2 * ph + n
        t_gram = t_ring_tok[(ph, NXT - 1)]
        W("vector", "t", t_gram)
        if it >= 1:
            W("vector", "t", t_attT_l[it - 1])
        em("vector", lambda eng, n=n: eng.reduce_max(
            vstat[:, 4:5], ps_g[n][:, :], AX.X, negate=True).then_inc(sem["v"], 1))
        v_max = iv("v")
        W("scalar", "v", v_max)
        if it > 0:
            W("scalar", "v", v_att[it - 1])
        em("scalar", lambda eng, n=n: eng.activation(
            scratch[:, :], ps_g[n][:, :], AF.Exp,
            bias=vstat[:, 4:5], accum_out=astat[:, 2:3]).then_inc(sem["a"], 1))
        a_exp[it] = iv("a")
        W("vector", "a", a_exp[it])
        em("vector", lambda eng: eng.reciprocal(
            vstat[:, 5:6], astat[:, 2:3]).then_inc(sem["v"], 1))
        W("vector", "v", iv("v"))
        em("vector", lambda eng: eng.tensor_scalar_mul(
            att_sb[:, 0:D], scratch[:, :], vstat[:, 5:6]).then_inc(sem["v"], 1))
        v_att[it] = iv("v")

        nonlocal_tp = {"c": 0}
        W("tensor", "v", v_att[it])
        W("tensor", "a", a_exp[it])
        ps_tpg = [ps_g[n][:, 0:512], ps_g[n][:, 512:1024]]
        if it >= 1:
            W("scalar", att_dBn[it - 1][0], att_dBn[it - 1][1])
        for jt in range(DT):
            pp = tp_count_ref[0] % 2
            W("tensor", "a", tp_evacs[tp_count_ref[0] - 2])
            em("tensor", lambda eng, jt=jt, pp=pp, n=n: eng.matmul(
                ps_tpg[pp][:, 0:128],
                att_sb[:, jt * 128: jt * 128 + 128],
                ident_sb[:, :], start=True, stop=True).then_inc(sem["t"], 1))
            tv = iv("t")
            W("scalar", "t", tv)
            em("scalar", lambda eng, jt=jt, pp=pp: eng.copy(
                attTit[:, jt * 128:(jt + 1) * 128],
                ps_tpg[pp][:, 0:128]).then_inc(sem["a"], 1))
            tp_evacs.append(iv("a"))
            tp_count_ref[0] += 1
        t_attT_l[it] = VAL["t"]
        a_att_ev = VAL["a"]

        W("gpsimd", "a", a_att_ev)
        att_dBn[it] = dma("gpsimd", "dB0" if it % 2 == 0 else "dB1",
                          lambda eng, it=it: eng.dma_start(
                              bass.AP(cc_att_in, it * 128, [[512, 128], [128 * 512, DT], [1, 128]]),
                              bass.AP(attTit, 0, [[DT * 128, 128], [128, DT], [1, 128]])))

    tp_count_ref = [tp_count]

    emit_gram_pass(0)

    while gamma_emitted[0] < NWT:
        emit_gamma_load()
    W("vector", "v", v_gred[10])
    em("vector", lambda eng: eng.tensor_reduce(
        gsum[:, 0:1], gacc[:, 0:11], AX.X, ALU.add).then_inc(sem["v"], 1))
    iv("v")
    W("vector", "v", v_gred[NWT - 1])
    em("vector", lambda eng: eng.tensor_reduce(
        gsum[:, 1:2], gacc[:, 11:17], AX.X, ALU.add).then_inc(sem["v"], 1))
    v_gs1 = iv("v")
    W("gpsimd", "v", v_gs1)
    t1 = dma("gpsimd", "dGF", lambda eng: eng.dma_start(
        gscr[0:1, :], gsum[:, 0:1], allow_slow_non_contiguous=True))
    t2 = dma("gpsimd", "dGF", lambda eng: eng.dma_start(
        gscr[1:2, :], gsum[:, 1:2], allow_slow_non_contiguous=True))
    WT("gpsimd", t2)
    t3 = dma("gpsimd", "dGF", lambda eng: eng.dma_start(
        gbc[:, :], bass.AP(gscr, 0, [[0, 128], [1, 256]])))
    WT("vector", t3)
    em("vector", lambda eng: eng.reduce_sum(
        vstat[:, 6:7], gbc[:, 0:128], AX.X).then_inc(sem["v"], 1))
    em("vector", lambda eng: eng.reduce_sum(
        vstat[:, 7:8], gbc[:, 128:256], AX.X).then_inc(sem["v"], 1))
    W("vector", "v", iv("v", 2))
    em("vector", lambda eng: eng.tensor_copy(gtmp[0:1, 0:1], vstat[0:1, 6:7]))
    em("vector", lambda eng: eng.tensor_copy(gtmp[0:1, 1:2], vstat[0:1, 7:8]).then_inc(sem["v"], 1))
    v_gt = iv("v")
    W("gpsimd", "v", v_gt)
    t4 = dma("gpsimd", "dGF", lambda eng: eng.dma_start(cc_g_in[0:1, :], gtmp[0:1, 0:2]))
    WT("gpsimd", t4)
    em("gpsimd", lambda eng: eng.collective_compute(
        "AllGather", ALU.bypass,
        replica_groups=[list(range(8))],
        ins=[cc_g_in.ap().opt()], outs=[cc_g_out.ap().opt()]).then_inc(sem["c"]))
    c_g = iv("c")
    W("gpsimd", "c", c_g)
    t5 = dma("gpsimd", "dGF", lambda eng: eng.dma_start(
        gag[0:1, :], bass.AP(cc_g_out, 0, [[16, 1], [1, 16]])))
    WT("vector", t5)
    W("vector", "dMi", 32)
    em("vector", lambda eng: eng.tensor_reduce(
        gtmp[0:1, 2:3], bass.AP(gag, 0, [[16, 1], [2, 4]]), AX.X, ALU.add))
    em("vector", lambda eng: eng.tensor_reduce(
        gtmp[0:1, 3:4], bass.AP(gag, 8, [[16, 1], [2, 4]]), AX.X, ALU.add))
    em("vector", lambda eng: eng.tensor_reduce(
        gtmp[0:1, 4:5], bass.AP(gag, 1, [[16, 1], [2, 8]]), AX.X, ALU.add).then_inc(sem["v"], 1))
    W("vector", "v", iv("v"))
    em("vector", lambda eng: eng.tensor_scalar(
        gtmp[0:1, 5:6], gtmp[0:1, 2:3], 0.5 / NELEM, 0.5e-8, ALU.mult, ALU.add))
    em("vector", lambda eng: eng.tensor_scalar(
        gtmp[0:1, 6:7], gtmp[0:1, 3:4], 0.5 / NELEM, 0.5e-8, ALU.mult, ALU.add))
    em("vector", lambda eng: eng.tensor_scalar(
        gtmp[0:1, 7:8], gtmp[0:1, 4:5], 0.5 / NELEM, 0.5e-8, ALU.mult, ALU.add).then_inc(sem["v"], 1))
    v_cv = iv("v")
    W("gpsimd", "v", v_cv)
    t6 = dma("gpsimd", "dGF", lambda eng: eng.dma_start(cscr[0:1, 0:3], gtmp[0:1, 5:8]))
    WT("gpsimd", t6)
    t7 = dma("gpsimd", "dGF", lambda eng: eng.dma_start(
        cb2[:, 0:3], bass.AP(cscr, 0, [[0, 128], [1, 3]])))
    WT("vector", t7)
    em("vector", lambda eng: eng.tensor_scalar_mul(
        ncb2[:, 0:3], cb2[:, 0:3], -1.0).then_inc(sem["v"], 1))
    v_cready = iv("v")
    W("gpsimd", "v", v_cready)

    emit_softmax_attT(0, 0)
    emit_softmax_attT(0, 1)

    vunits = [(w1, j * 128, 0, D, 0, q1) for j in range(HT)]
    for j3 in range(8):
        for (c0, wdt) in W3C:
            vunits.append((w3, j3 * 128, c0, wdt, 2, q3))
    bunits = [(w2, j * 128, 0, D, 1, q2) for j in range(HT)]
    for j3 in range(8, DT):
        for (c0, wdt) in W3C:
            bunits.append((w3, j3 * 128, c0, wdt, 2, q3))

    NVU, NGU = len(vunits), len(bunits)
    v_q = [0] * NVU
    tok_ql = [None] * NVU
    tok_qw = [None] * NVU
    g_q = [0] * NGU
    a_sg_l = [0] * NGU
    tok_q2l = [None] * NGU
    tok_q2w = [None] * NGU
    qlsem = ["dQl0", "dQl1"]
    qwsem = ["dQw0", "dQw1"]
    g2sem = ["dG0", "dG1"]
    vq_iter = iter(range(NVU))
    gq_iter = iter(range(NGU))

    def emit_vq(count):
        for _ in range(count):
            try:
                ui = next(vq_iter)
            except StopIteration:
                return
            wd, r0, c0, wdt, cidx, qd = vunits[ui]
            sl = (ui % 2) * D
            W("scalar", "v", v_gred[NWT - 1] if ui < 2 else v_q[ui - 2])
            tok_ql[ui] = dma("scalar", qlsem[ui % 2],
                             lambda eng, wd=wd, r0=r0, c0=c0, wdt=wdt, sl=sl: eng.dma_start(
                                 ring[:, sl:sl + wdt], wd[r0:r0 + 128, c0:c0 + wdt]))
            WT("vector", tok_ql[ui])
            if ui == 0:
                W("vector", "v", v_cready)
            qp = ui % 2
            if ui >= 2:
                WT("vector", tok_qw[ui - 2])
            if ui >= 1:
                W("vector", "v", v_q[ui - 1])
            em("vector", lambda eng, sl=sl, wdt=wdt, cidx=cidx: eng.tensor_single_scalar(
                lt_sb[:, 0:wdt], ring[:, sl:sl + wdt], ncb2[:, cidx:cidx + 1],
                ALU.is_lt).then_inc(sem["v"], 1))
            W("vector", "v", iv("v"))
            em("vector", lambda eng, sl=sl, wdt=wdt, cidx=cidx, qp=qp: eng.scalar_tensor_tensor(
                qt_sb[:, qp * D: qp * D + wdt], ring[:, sl:sl + wdt], cb2[:, cidx:cidx + 1],
                lt_sb[:, 0:wdt], ALU.is_gt, ALU.subtract).then_inc(sem["v"], 1))
            v_q[ui] = iv("v")
            W("scalar", "v", v_q[ui])
            tok_qw[ui] = dma("scalar", qwsem[qp],
                             lambda eng, qd=qd, r0=r0, c0=c0, wdt=wdt, qp=qp: eng.dma_start(
                                 qd[r0:r0 + 128, c0:c0 + wdt], qt_sb[:, qp * D: qp * D + wdt]))

    def emit_gq(count):
        for _ in range(count):
            try:
                ui = next(gq_iter)
            except StopIteration:
                return
            wd, r0, c0, wdt, cidx, qd = bunits[ui]
            sl = (ui % 2) * D
            if ui >= 2:
                W("scalar", "a", a_sg_l[ui - 2])
            tok_q2l[ui] = dma("scalar", g2sem[ui % 2],
                              lambda eng, wd=wd, r0=r0, c0=c0, wdt=wdt, sl=sl: eng.dma_start(
                                  ring2[:, sl:sl + wdt], wd[r0:r0 + 128, c0:c0 + wdt]))
            WT("scalar", tok_q2l[ui])
            if ui == 0:
                W("scalar", "v", v_cready)
            if ui >= 1:
                W("scalar", "v", g_q[ui - 1])
            em("scalar", lambda eng, sl=sl, wdt=wdt, cidx=cidx: eng.activation(
                s12_sb[:, 0:wdt], ring2[:, sl:sl + wdt], AF.Sign,
                bias=ncb2[:, cidx:cidx + 1]))
            em("scalar", lambda eng, sl=sl, wdt=wdt, cidx=cidx: eng.activation(
                s12_sb[:, D:D + wdt], ring2[:, sl:sl + wdt], AF.Sign,
                bias=cb2[:, cidx:cidx + 1]).then_inc(sem["a"], 1))
            a_sg = iv("a")
            a_sg_l[ui] = a_sg
            qp = ui % 2
            W("vector", "a", a_sg)
            if ui >= 2:
                WT("vector", tok_q2w[ui - 2])
            em("vector", lambda eng, wdt=wdt, qp=qp: eng.tensor_tensor(
                qt2_sb[:, qp * D: qp * D + wdt], s12_sb[:, 0:wdt],
                s12_sb[:, D:D + wdt], ALU.add).then_inc(sem["v"], 1))
            g_q[ui] = iv("v")
            W("scalar", "v", g_q[ui])
            tok_q2w[ui] = dma("scalar", "dQ2" if qp == 0 else "dQ2b",
                              lambda eng, qd=qd, r0=r0, c0=c0, wdt=wdt, qp=qp: eng.dma_start(
                                  qd[r0:r0 + 128, c0:c0 + wdt], qt2_sb[:, qp * D: qp * D + wdt]))

    emit_vq(22)
    emit_gq(12)

    emit_gram_pass(1)
    emit_softmax_attT(1, 0)
    emit_softmax_attT(1, 1)

    W("gpsimd", "dB0", VAL["dB0"])
    W("gpsimd", "dB1", VAL["dB1"])
    em("gpsimd", lambda eng: eng.collective_compute(
        "AllGather", ALU.bypass,
        replica_groups=[[0, 1, 2, 3], [4, 5, 6, 7]],
        ins=[cc_att_in.ap().opt()], outs=[cc_att_out.ap().opt()]).then_inc(sem["c"]))
    c_att = iv("c")

    emit_vq(NVU)
    emit_gq(8)

    atsem = ["dAT0", "dAT1"]
    ps_o = [ps_all[:, 3072:3584], ps_all[:, 3584:4096]]
    asem = ["dA0", "dA1"]
    t_ao_blk = [0] * 4
    v_aoe = {}
    tok_ao = {}
    ao_seq = []
    W("sync", "c", c_att)
    atok = [None, None]
    for r in range(4):
        pb = r % 2
        if r >= 1:
            W("sync", "t", t_ao_blk[r - 1])
        atok[pb] = dma("sync", atsem[pb], lambda eng, r=r: eng.dma_start(
            bass.AP(attTf, 0, [[DT * 512, 128], [512, DT], [1, 512]]),
            bass.AP(cc_att_out, r * D * 512, [[512, 128], [128 * 512, DT], [1, 512]])))
        if r == 0:
            W("tensor", "a", tp_evacs[-1])
            W("tensor", "a", a_exp[NIT - 1])
        WT("tensor", atok[pb])
        for tt in range(NLT):
            pp = tt % 2
            key = (r, tt)
            if len(ao_seq) >= 2:
                W("tensor", "v", v_aoe[ao_seq[-2]])
            for jt in range(DT):
                st, sp = (jt == 0), (jt == DT - 1)
                def mmo(eng, tt=tt, jt=jt, pp=pp, st=st, sp=sp):
                    return eng.matmul(
                        ps_o[pp][:, 0:512],
                        xnT_sb[:, jt * T + tt * 128: jt * T + tt * 128 + 128],
                        attTf[:, jt * 512:(jt + 1) * 512],
                        start=st, stop=sp)
                if sp:
                    em("tensor", lambda eng, mmo=mmo: mmo(eng).then_inc(sem["t"], 1))
                else:
                    em("tensor", mmo)
            t_ao = iv("t")
            W("vector", "t", t_ao)
            if len(ao_seq) >= 2:
                WT("vector", tok_ao[ao_seq[-2]])
            em("vector", lambda eng, pp=pp: eng.tensor_copy(
                stage[:, pp * 512:(pp + 1) * 512], ps_o[pp][:, 0:512]).then_inc(sem["v"], 1))
            v_aoe[key] = iv("v")
            W("gpsimd", "v", v_aoe[key])
            if not ao_seq:
                WT("gpsimd", tok_outinit)
            tok_ao[key] = dma("gpsimd", asem[pp], lambda eng, r=r, tt=tt, pp=pp: eng.dma_start(
                out[tt * 128:(tt + 1) * 128, r * 512:(r + 1) * 512],
                stage[:, pp * 512:(pp + 1) * 512],
                accum_op=ALU.add))
            ao_seq.append(key)
            emit_gq(1)
        t_ao_blk[r] = VAL["t"]
    emit_gq(NGU)

    barA_t = VAL["t"]
    barA_a = VAL["a"]
    barA_v = VAL["v"]
    for engn in ("sync", "tensor", "scalar", "vector", "gpsimd"):
        W(engn, "t", barA_t)
        W(engn, "a", barA_a)
        W(engn, "v", barA_v)
        WT(engn, tok_ao[(3, NLT - 1)])
        WT(engn, tok_ao[(3, NLT - 2)])
        for s in ("dQw0", "dQw1", "dQ2", "dQ2b", "dB0", "dB1", "dBn", "dGF"):
            W(engn, s, VAL[s])
    phA.close()

    b0 = open_stack()
    hT_sb = b0.enter_context(nc.sbuf_tensor("hT_sb", [128, HT * T], BF16, side="left"))
    phB = open_stack()
    wq_sb = phB.enter_context(nc.sbuf_tensor("wq_sb", [128, 2 * DT * 1024], BF16, side="left"))
    ps_x = [ps_all[:, 0:1024], ps_all[:, 1024:2048]]

    ffn_state = {"unit": 0, "ev": [], "sig_ev": [0] * HT, "wins": [], "wtok": []}
    wlsem = ["dWa", "dWb", "dWc", "dWd"]

    def ffn12(qd, kind, tokq):
        for si, (h0, sh) in enumerate(SPANS12):
            wi = len(ffn_state["wins"])
            buf = (wi % 2) * DT * 1024
            queue = "sync"
            wsem = wlsem[wi % 2 + (0 if kind == "sig" else 2)]
            lt_i = (h0 + sh) // 128 - 1
            WT(queue, tokq[lt_i])
            if lt_i >= 1:
                WT(queue, tokq[lt_i - 1])
            if wi >= 2:
                W(queue, "t", ffn_state["wins"][wi - 2])
            last_tok = None
            for K in range(DT):
                def ldw(eng, qd=qd, h0=h0, sh=sh, K=K, buf=buf):
                    return eng.dma_start(
                        bass.AP(wq_sb, buf + K * 1024, [[2 * DT * 1024, 128], [1, sh]]),
                        qd[h0:h0 + sh, K * 128:(K + 1) * 128],
                        transpose=True)
                last_tok = dma(queue, wsem, ldw)
            ffn_state["wtok"].append(last_tok)
            WT("tensor", last_tok)
            for mt in range(sh // 128):
                slab = h0 // 128 + mt
                unit = ffn_state["unit"]
                pp = unit % 2
                if unit >= 2:
                    sname, sval = ffn_state["ev"][unit - 2]
                    W("tensor", sname, sval)
                for K in range(DT):
                    for c in range(2):
                        def mmf(eng, buf=buf, K=K, mt=mt, c=c, pp=pp):
                            return eng.matmul(
                                ps_x[pp][:, c * 512:(c + 1) * 512],
                                wq_sb[:, buf + K * 1024 + mt * 128: buf + K * 1024 + mt * 128 + 128],
                                xnT_sb[:, K * T + c * 512: K * T + (c + 1) * 512],
                                start=(K == 0), stop=(K == DT - 1))
                        if K == DT - 1 and c == 1:
                            em("tensor", lambda eng, mmf=mmf: mmf(eng).then_inc(sem["t"], 1))
                        else:
                            em("tensor", mmf)
                tmm = iv("t")
                if mt == sh // 128 - 1:
                    ffn_state["wins"].append(tmm)
                if kind == "sig":
                    W("scalar", "t", tmm)
                    em("scalar", lambda eng, slab=slab, pp=pp: eng.activation(
                        hT_sb[:, slab * T:(slab + 1) * T], ps_x[pp][:, :],
                        AF.Sigmoid).then_inc(sem["a"], 1))
                    av = iv("a")
                    ffn_state["ev"].append(("a", av))
                    ffn_state["sig_ev"][slab] = av
                else:
                    W("vector", "t", tmm)
                    W("vector", "a", ffn_state["sig_ev"][slab])
                    em("vector", lambda eng, slab=slab, pp=pp: eng.scalar_tensor_tensor(
                        hT_sb[:, slab * T:(slab + 1) * T],
                        ps_x[pp][:, :], 0.5,
                        hT_sb[:, slab * T:(slab + 1) * T],
                        ALU.mult, ALU.mult).then_inc(sem["v"], 1))
                    ffn_state["ev"].append(("v", iv("v")))
                ffn_state["unit"] += 1

    tokq1 = [None] * HT
    tokq2 = [None] * HT
    tokq3 = [None] * DT
    for ui, (wd, r0, c0, wdt, cidx, qd) in enumerate(vunits):
        if qd is q1:
            tokq1[r0 // 128] = tok_qw[ui]
        else:
            tokq3[r0 // 128] = tok_qw[ui]
    for ui, (wd, r0, c0, wdt, cidx, qd) in enumerate(bunits):
        if qd is q2:
            tokq2[r0 // 128] = tok_q2w[ui]
        else:
            tokq3[r0 // 128] = tok_q2w[ui]

    ffn12(q1, "sig", tokq1)
    ffn12(q2, "mul", tokq2)

    barB = dict(VAL)
    for engn in ("sync", "vector", "scalar", "tensor", "gpsimd"):
        for s in SEMS:
            W(engn, s, barB[s])
    phB.close()
    abx.close()

    phC = open_stack()
    ec = phC.enter_context
    w3T_sb = ec(nc.sbuf_tensor("w3T_sb", [128, 2 * HT * 512], BF16))
    fstage = ec(nc.sbuf_tensor("fstage", [128, 2 * 512], F32))
    ps_f = [ps_all[:, 0:512], ps_all[:, 512:1024]]

    fsem = ["dF0", "dF1"]
    wsem3 = ["dWa", "dWb"]
    t3_cons = [0] * 4
    f_seq = []
    v_fe = {}
    tok_f = {}
    for db in range(4):
        dout0 = SPANS3[db]
        buf = (db % 2) * HT * 512
        queue = "sync"
        lt3 = (dout0 + 512) // 128 - 1
        WT(queue, tokq3[lt3])
        if lt3 >= 1:
            WT(queue, tokq3[lt3 - 1])
        if db >= 2:
            W(queue, "t", t3_cons[db - 2])
        last_tok = None
        for K in range(HT):
            def ld3(eng, dout0=dout0, K=K, buf=buf):
                return eng.dma_start(
                    bass.AP(w3T_sb, buf + K * 512, [[2 * HT * 512, 128], [1, 512]]),
                    q3[dout0:dout0 + 512, K * 128:(K + 1) * 128],
                    transpose=True)
            last_tok = dma(queue, wsem3[db % 2], ld3)
        WT("tensor", last_tok)
        for mt in range(NLT):
            pp = len(f_seq) % 2
            key = (db, mt)
            if len(f_seq) >= 2:
                W("tensor", "v", v_fe[f_seq[-2]])
            for K in range(HT):
                def mm3(eng, buf=buf, K=K, mt=mt, pp=pp):
                    return eng.matmul(
                        ps_f[pp][:, 0:512],
                        hT_sb[:, K * T + mt * 128: K * T + mt * 128 + 128],
                        w3T_sb[:, buf + K * 512: buf + (K + 1) * 512],
                        start=(K == 0), stop=(K == HT - 1))
                if K == HT - 1:
                    em("tensor", lambda eng, mm3=mm3: mm3(eng).then_inc(sem["t"], 1))
                else:
                    em("tensor", mm3)
            t_f = iv("t")
            if mt == NLT - 1:
                t3_cons[db] = t_f
            W("vector", "t", t_f)
            if len(f_seq) >= 2:
                WT("vector", tok_f[f_seq[-2]])
            if db >= 2:
                em("vector", lambda eng, pp=pp: eng.tensor_scalar_mul(
                    fstage[:, pp * 512:(pp + 1) * 512], ps_f[pp][:, 0:512],
                    0.5).then_inc(sem["v"], 1))
            else:
                em("vector", lambda eng, pp=pp: eng.tensor_copy(
                    fstage[:, pp * 512:(pp + 1) * 512], ps_f[pp][:, 0:512]).then_inc(sem["v"], 1))
            v_fe[key] = iv("v")
            W("gpsimd", "v", v_fe[key])
            tok_f[key] = dma("gpsimd", fsem[pp], lambda eng, dout0=dout0, mt=mt, pp=pp:
                             eng.dma_start(
                                 out[mt * 128:(mt + 1) * 128, dout0:dout0 + 512],
                                 fstage[:, pp * 512:(pp + 1) * 512],
                                 accum_op=ALU.add))
            f_seq.append(key)

    for s in DMA_SEMS:
        W("sync", s, VAL[s])
        W("gpsimd", s, VAL[s])

    with nc.Block() as block:
        @block.sync
        def _(eng):
            for f in streams["sync"]:
                f(eng)

        @block.vector
        def _(eng):
            for f in streams["vector"]:
                f(eng)

        @block.scalar
        def _(eng):
            for f in streams["scalar"]:
                f(eng)

        @block.tensor
        def _(eng):
            for f in streams["tensor"]:
                f(eng)

        @block.gpsimd
        def _(eng):
            for f in streams["gpsimd"]:
                f(eng)

    phC.close()
    b0.close()
    top.close()
    return nc


_EYE = np.eye(128, dtype=ml_dtypes.bfloat16)
_CONSTS = np.concatenate([
    np.full((128, 1), GBIAS, np.float32),
    np.full((128, 1), LNN, np.float32)], axis=1)


def make_in_maps(x, w1, w2, w3):
    in_maps = []
    for k in range(8):
        b, q = k // 4, k % 4
        t0 = q * 1024
        wslk = w1[k * HSL:(k + 1) * HSL] if k < 4 else w2[(k - 4) * HSL:(k - 3) * HSL]
        in_maps.append({
            "xloc": np.ascontiguousarray(x[b, t0:t0 + 1024], dtype=np.float32),
            "wsl": np.ascontiguousarray(wslk, dtype=np.float32),
            "w3sl": np.ascontiguousarray(w3[k * W3R:(k + 1) * W3R], dtype=np.float32),
            "w1": np.ascontiguousarray(w1, dtype=np.float32),
            "w2": np.ascontiguousarray(w2, dtype=np.float32),
            "w3": np.ascontiguousarray(w3, dtype=np.float32),
            "ident": _EYE,
            "consts": _CONSTS,
            "ioff": np.array([[q * 512]], dtype=np.uint32),
            "grpflag": np.array([[1.0, 0.0]] if k < 4 else [[0.0, 1.0]], dtype=np.float32),
        })
    return in_maps


def kernel(L0_attn_x, L0_ffn_w1, L0_ffn_w2, L0_ffn_w3):
    x = np.asarray(L0_attn_x, dtype=np.float32)
    w1 = np.asarray(L0_ffn_w1, dtype=np.float32)
    w2 = np.asarray(L0_ffn_w2, dtype=np.float32)
    w3 = np.asarray(L0_ffn_w3, dtype=np.float32)
    nc = build_nc()
    res = run_bass_kernel_spmd(nc, make_in_maps(x, w1, w2, w3), core_ids=list(range(8)))
    outp = np.empty((2, 4096, 2048), np.float32)
    for k, r in enumerate(res.results):
        outp[k // 4, (k % 4) * 1024:((k % 4) + 1) * 1024] = r["out"]
    return outp


# version 09
# speedup vs baseline: 1.4926x; 1.2882x over previous
import math
import numpy as np
import ml_dtypes
from contextlib import ExitStack

import concourse.bass as bass
import concourse.mybir as mybir
from concourse.bass_utils import run_bass_kernel_spmd

F32 = mybir.dt.float32
BF16 = mybir.dt.bfloat16
U32 = mybir.dt.uint32
AF = mybir.ActivationFunctionType
ALU = mybir.AluOpType
AX = mybir.AxisListType

S, D, H, T = 4096, 2048, 2048 + 3584, 1024
H = 5632
NLT = T // 128
NXT = S // 128
DT = D // 128
HT = H // 128
NIT = 4
HSL = H // 4
W3R = D // 8
NELEM = H * D
GBIAS = float(D) * float(D) * 1e-5
LNN = math.log(float(D))
W3C = [(0, 2048), (2048, 2048), (4096, 1536)]
SP = 512
NPAIR = H // SP
NSLAB = SP // 128

SEMS = ("v", "a", "t", "c",
        "dId", "dCs", "dMi", "dO", "dGF", "dBn", "dB0", "dB1", "dXI",
        "dX0", "dX1", "dN0", "dN1", "dAT0", "dAT1",
        "dQl0", "dQl1", "dQw0", "dQw1",
        "dWa", "dWb", "dH0", "dH1", "dH2", "dH3", "dHL0", "dHL1",
        "dA0", "dA1", "dF0", "dF1")
DMA_SEMS = tuple(s for s in SEMS if s.startswith("d"))


def build_nc():
    nc = bass.Bass("TRN2", target_bir_lowering=False)

    xloc = nc.dram_tensor("xloc", [T, D], F32, kind="ExternalInput")
    wsl = nc.dram_tensor("wsl", [HSL, D], F32, kind="ExternalInput")
    w3sl = nc.dram_tensor("w3sl", [W3R, H], F32, kind="ExternalInput")
    w1 = nc.dram_tensor("w1", [H, D], F32, kind="ExternalInput")
    w2 = nc.dram_tensor("w2", [H, D], F32, kind="ExternalInput")
    w3 = nc.dram_tensor("w3", [D, H], F32, kind="ExternalInput")
    ident = nc.dram_tensor("ident", [128, 128], BF16, kind="ExternalInput")
    consts = nc.dram_tensor("consts", [128, 2], F32, kind="ExternalInput")
    ioff = nc.dram_tensor("ioff", [1, 2], U32, kind="ExternalInput")
    out = nc.dram_tensor("out", [T, D], F32, kind="ExternalOutput")

    q1 = nc.dram_tensor("q1", [H, D], BF16, kind="Internal")
    q2 = nc.dram_tensor("q2", [H, D], BF16, kind="Internal")
    q3 = nc.dram_tensor("q3", [D, H], BF16, kind="Internal")
    hid_d = nc.dram_tensor("hid_d", [H, T], BF16, kind="Internal")
    cc_xn_in = nc.dram_tensor("cc_xn_in", [T, D], BF16, kind="Internal")
    cc_xn_out = nc.dram_tensor("cc_xn_out", [S, D], BF16, kind="Internal")
    cc_att_in = nc.dram_tensor("cc_att_in", [1, D * 512], BF16, kind="Internal")
    cc_att_out = nc.dram_tensor("cc_att_out", [4, D * 512], BF16, kind="Internal")
    cc_g_in = nc.dram_tensor("cc_g_in", [1, 2], F32, kind="Internal")
    cc_g_out = nc.dram_tensor("cc_g_out", [8, 2], F32, kind="Internal")
    gscr = nc.dram_tensor("gscr", [2, 128], F32, kind="Internal")
    cscr = nc.dram_tensor("cscr", [1, 4], F32, kind="Internal")

    streams = {e: [] for e in ("sync", "vector", "scalar", "tensor", "gpsimd")}
    VAL = {s: 0 for s in SEMS}

    def em(engine, fn):
        streams[engine].append(fn)

    def iv(s, amt=1):
        VAL[s] += amt
        return VAL[s]

    stacks = []

    def open_stack():
        st = ExitStack()
        stacks.append(st)
        return st

    top = open_stack()
    sem = {n: top.enter_context(nc.semaphore(n)) for n in SEMS}

    def W(engine, s, val):
        if val and val > 0:
            em(engine, lambda eng, s=s, val=val: eng.wait_ge(sem[s], val))

    def WT(engine, tok):
        if tok is not None:
            W(engine, tok[0], tok[1])

    def dma(engine, stream, build):
        v = iv(stream, 16)
        em(engine, lambda eng, build=build, stream=stream: build(eng).then_inc(sem[stream], 16))
        return (stream, v)

    e = top.enter_context
    ident_sb = e(nc.sbuf_tensor("ident_sb", [128, 128], BF16))
    csb = e(nc.sbuf_tensor("csb", [128, 2], F32))
    ioffs = e(nc.sbuf_tensor("ioffs", [1, 2], U32))
    gacc = e(nc.sbuf_tensor("gacc", [128, 24], F32))
    gsum = e(nc.sbuf_tensor("gsum", [128, 2], F32))
    gbc = e(nc.sbuf_tensor("gbc", [128, 256], F32))
    gag = e(nc.sbuf_tensor("gag", [1, 16], F32))
    gtmp = e(nc.sbuf_tensor("gtmp", [1, 8], F32))
    cb2 = e(nc.sbuf_tensor("cb2", [128, 4], F32))
    ncb2 = e(nc.sbuf_tensor("ncb2", [128, 4], F32))
    vstat = e(nc.sbuf_tensor("vstat", [128, 8], F32))
    astat = e(nc.sbuf_tensor("astat", [128, 8], F32))
    ps_all = e(nc.psum_tensor("ps_all", [128, 4096], F32))

    tok_ident = dma("sync", "dId", lambda eng: eng.dma_start(ident_sb[:, :], ident[:, :]))
    tok_csb = dma("sync", "dCs", lambda eng: eng.dma_start(csb[:, :], consts[:, :]))
    dma("sync", "dMi", lambda eng: eng.dma_start(ioffs[0:1, 0:2], ioff[0:1, 0:2]))
    tok_outinit = dma("gpsimd", "dO", lambda eng: eng.dma_start(out[:, :], xloc[:, :]))

    xnT_sb = e(nc.sbuf_tensor("xnT_sb", [128, DT * T], BF16, side="left"))
    stage = e(nc.sbuf_tensor("stage", [128, 2 * 512], F32, side="left"))
    phF = open_stack()
    wq_sb = phF.enter_context(nc.sbuf_tensor("wq_sb", [128, 2 * DT * SP], BF16, side="left"))
    sstg = phF.enter_context(nc.sbuf_tensor("sstg", [128, 2 * NSLAB * T], BF16, side="left"))
    hstg = phF.enter_context(nc.sbuf_tensor("hstg", [128, 4 * T], BF16, side="left"))
    phA1 = open_stack()
    ea1 = phA1.enter_context
    nring = ea1(nc.sbuf_tensor("nring", [128, 2 * 2 * D], BF16, side="left"))
    xn_i = ea1(nc.sbuf_tensor("xn_i", [128, NXT * 256], BF16, side="left"))
    scratch = ea1(nc.sbuf_tensor("scratch", [128, D], BF16, side="left"))
    att_sb = ea1(nc.sbuf_tensor("att_sb", [128, D], BF16, side="left"))

    abx = open_stack()
    ring = abx.enter_context(nc.sbuf_tensor("ring", [128, 2 * D], F32, side="right"))
    lt_sb = abx.enter_context(nc.sbuf_tensor("lt_sb", [128, D], BF16, side="right"))
    qt_sb = abx.enter_context(nc.sbuf_tensor("qt_sb", [128, 2 * D], BF16, side="right"))
    phXL = open_stack()
    xn_loc = phXL.enter_context(nc.sbuf_tensor("xn_loc", [128, NLT * D], BF16, side="right"))

    rsem = ["dX0", "dX1"]
    nsem = ["dN0", "dN1"]

    v_xn = [0] * NLT
    a_acc = [0] * NLT
    for i in range(NLT):
        sl = (i % 2) * D
        if i >= 2:
            W("sync", "v", v_xn[i - 2])
        tok = dma("sync", rsem[i % 2], lambda eng, i=i, sl=sl: eng.dma_start(
            ring[:, sl:sl + D], xloc[i * 128:(i + 1) * 128, :]))
        if i == 0:
            WT("scalar", tok_csb)
        WT("scalar", tok)
        if i >= 2:
            W("scalar", "v", v_xn[i - 2])
        em("scalar", lambda eng, sl=sl, i=i: eng.activation(
            scratch[:, :], ring[:, sl:sl + D], AF.Copy,
            accum_out=astat[:, 4 + (i % 2):5 + (i % 2)]).then_inc(sem["a"], 1))
        a_s1 = iv("a")
        WT("vector", tok)
        em("vector", lambda eng, sl=sl: eng.scalar_tensor_tensor(
            att_sb[:, 0:D], ring[:, sl:sl + D], 1.0, ring[:, sl:sl + D],
            ALU.mult, ALU.mult, accum_out=vstat[:, 1:2]).then_inc(sem["v"], 1))
        W("vector", "v", iv("v"))
        W("vector", "a", a_s1)
        em("vector", lambda eng, i=i: eng.tensor_tensor(
            vstat[:, 3:4], astat[:, 4 + (i % 2):5 + (i % 2)],
            astat[:, 4 + (i % 2):5 + (i % 2)], ALU.mult))
        em("vector", lambda eng, i=i: eng.tensor_scalar_mul(
            vstat[:, 2:3], astat[:, 4 + (i % 2):5 + (i % 2)], 1.0 / D).then_inc(sem["v"], 1))
        W("vector", "v", iv("v"))
        em("vector", lambda eng: eng.scalar_tensor_tensor(
            vstat[:, 4:5], vstat[:, 1:2], float(D), vstat[:, 3:4],
            ALU.mult, ALU.subtract).then_inc(sem["v"], 1))
        h3 = iv("v")
        W("scalar", "v", h3)
        em("scalar", lambda eng: eng.activation(
            astat[:, 0:1], vstat[:, 4:5], AF.Ln, bias=csb[:, 0:1]).then_inc(sem["a"], 1))
        W("scalar", "a", iv("a"))
        em("scalar", lambda eng: eng.activation(
            astat[:, 1:2], astat[:, 0:1], AF.Exp, scale=-0.5, bias=csb[:, 1:2]).then_inc(sem["a"], 1))
        a_rstd = iv("a")
        W("vector", "a", a_rstd)
        em("vector", lambda eng, i=i, sl=sl: eng.tensor_scalar(
            xn_loc[:, i * D:(i + 1) * D], ring[:, sl:sl + D],
            vstat[:, 2:3], astat[:, 1:2], ALU.subtract, ALU.mult).then_inc(sem["v"], 1))
        v_xn[i] = iv("v")
        a_acc[i] = a_s1

    wtile_specs = []
    for j in range(HSL // 128):
        wtile_specs.append((wsl, j * 128, 0, D, j))
    for j3 in range(2):
        for ci, (c0, wdt) in enumerate(W3C):
            wtile_specs.append((w3sl, j3 * 128, c0, wdt, 11 + 3 * j3 + ci))
    NWT = len(wtile_specs)
    v_gred = [0] * NWT
    for idx, (wd, r0, c0, wdt, gc) in enumerate(wtile_specs):
        sl = (idx % 2) * D
        W("sync", "v", v_xn[NLT - 1] if idx < 2 else v_gred[idx - 2])
        tok = dma("sync", rsem[idx % 2], lambda eng, wd=wd, r0=r0, c0=c0, wdt=wdt, sl=sl:
                  eng.dma_start(ring[:, sl:sl + wdt], wd[r0:r0 + 128, c0:c0 + wdt]))
        WT("vector", tok)
        em("vector", lambda eng, sl=sl, wdt=wdt, gc=gc: eng.tensor_reduce(
            gacc[:, gc:gc + 1], ring[:, sl:sl + wdt], AX.X, ALU.add,
            apply_absolute_value=True).then_inc(sem["v"], 1))
        v_gred[idx] = iv("v")

    W("vector", "v", v_gred[NWT - 1])
    em("vector", lambda eng: eng.tensor_reduce(
        gsum[:, 0:1], gacc[:, 0:11], AX.X, ALU.add))
    em("vector", lambda eng: eng.tensor_reduce(
        gsum[:, 1:2], gacc[:, 11:17], AX.X, ALU.add).then_inc(sem["v"], 1))
    v_gs1 = iv("v")

    W("gpsimd", "v", v_xn[NLT - 1])
    tok_xnw = dma("gpsimd", "dBn", lambda eng: eng.dma_start(
        bass.AP(cc_xn_in, 0, [[D, 128], [128 * D, NLT], [1, D]]),
        bass.AP(xn_loc, 0, [[NLT * D, 128], [D, NLT], [1, D]])))
    W("gpsimd", "v", v_gs1)
    t1 = dma("gpsimd", "dGF", lambda eng: eng.dma_start(
        gscr[0:1, :], gsum[:, 0:1], allow_slow_non_contiguous=True))
    t2 = dma("gpsimd", "dGF", lambda eng: eng.dma_start(
        gscr[1:2, :], gsum[:, 1:2], allow_slow_non_contiguous=True))
    WT("gpsimd", t2)
    t3 = dma("gpsimd", "dGF", lambda eng: eng.dma_start(
        gbc[:, :], bass.AP(gscr, 0, [[0, 128], [1, 256]])))
    WT("vector", t3)
    em("vector", lambda eng: eng.reduce_sum(
        vstat[:, 6:7], gbc[:, 0:128], AX.X))
    em("vector", lambda eng: eng.reduce_sum(
        vstat[:, 7:8], gbc[:, 128:256], AX.X).then_inc(sem["v"], 1))
    W("vector", "v", iv("v"))
    em("vector", lambda eng: eng.tensor_copy(gtmp[0:1, 0:1], vstat[0:1, 6:7]))
    em("vector", lambda eng: eng.tensor_copy(gtmp[0:1, 1:2], vstat[0:1, 7:8]).then_inc(sem["v"], 1))
    v_gt = iv("v")
    W("gpsimd", "v", v_gt)
    t4 = dma("gpsimd", "dGF", lambda eng: eng.dma_start(cc_g_in[0:1, :], gtmp[0:1, 0:2]))
    WT("gpsimd", t4)
    em("gpsimd", lambda eng: eng.collective_compute(
        "AllGather", ALU.bypass,
        replica_groups=[list(range(8))],
        ins=[cc_g_in.ap().opt()], outs=[cc_g_out.ap().opt()]).then_inc(sem["c"]))
    c_g = iv("c")
    WT("gpsimd", tok_xnw)
    em("gpsimd", lambda eng: eng.collective_compute(
        "AllGather", ALU.bypass,
        replica_groups=[[0, 1, 2, 3], [4, 5, 6, 7]],
        ins=[cc_xn_in.ap().opt()], outs=[cc_xn_out.ap().opt()]).then_inc(sem["c"]))
    c_xn = iv("c")
    W("gpsimd", "c", c_g)
    t5 = dma("gpsimd", "dGF", lambda eng: eng.dma_start(
        gag[0:1, :], bass.AP(cc_g_out, 0, [[16, 1], [1, 16]])))
    WT("vector", t5)
    em("vector", lambda eng: eng.tensor_reduce(
        gtmp[0:1, 2:3], bass.AP(gag, 0, [[16, 1], [2, 4]]), AX.X, ALU.add))
    em("vector", lambda eng: eng.tensor_reduce(
        gtmp[0:1, 3:4], bass.AP(gag, 8, [[16, 1], [2, 4]]), AX.X, ALU.add))
    em("vector", lambda eng: eng.tensor_reduce(
        gtmp[0:1, 4:5], bass.AP(gag, 1, [[16, 1], [2, 8]]), AX.X, ALU.add).then_inc(sem["v"], 1))
    W("vector", "v", iv("v"))
    em("vector", lambda eng: eng.tensor_scalar(
        gtmp[0:1, 5:6], gtmp[0:1, 2:3], 0.5 / NELEM, 0.5e-8, ALU.mult, ALU.add))
    em("vector", lambda eng: eng.tensor_scalar(
        gtmp[0:1, 6:7], gtmp[0:1, 3:4], 0.5 / NELEM, 0.5e-8, ALU.mult, ALU.add))
    em("vector", lambda eng: eng.tensor_scalar(
        gtmp[0:1, 7:8], gtmp[0:1, 4:5], 0.5 / NELEM, 0.5e-8, ALU.mult, ALU.add).then_inc(sem["v"], 1))
    v_cv = iv("v")
    W("gpsimd", "v", v_cv)
    t6 = dma("gpsimd", "dGF", lambda eng: eng.dma_start(cscr[0:1, 0:3], gtmp[0:1, 5:8]))
    WT("gpsimd", t6)
    t7 = dma("gpsimd", "dGF", lambda eng: eng.dma_start(
        cb2[:, 0:3], bass.AP(cscr, 0, [[0, 128], [1, 3]])))
    WT("vector", t7)
    em("vector", lambda eng: eng.tensor_scalar_mul(
        ncb2[:, 0:3], cb2[:, 0:3], -1.0).then_inc(sem["v"], 1))
    v_cready = iv("v")

    ps_tp = [ps_all[:, 2048:2560], ps_all[:, 2560:3072]]
    tpg_evac = []
    WT("tensor", tok_ident)
    W("tensor", "v", v_xn[NLT - 1])
    for g in range(32):
        tt, dt0 = g % NLT, (g // NLT) * 4
        pp = g % 2
        if g >= 2:
            W("tensor", "a", tpg_evac[g - 2])
        for j in range(4):
            last = (j == 3)
            def mmt(eng, tt=tt, dt=dt0, j=j, pp=pp):
                return eng.matmul(
                    ps_tp[pp][:, j * 128:(j + 1) * 128],
                    xn_loc[:, tt * D + (dt + j) * 128: tt * D + (dt + j) * 128 + 128],
                    ident_sb[:, :], start=True, stop=True)
            if last:
                em("tensor", lambda eng, mmt=mmt: mmt(eng).then_inc(sem["t"], 1))
            else:
                em("tensor", mmt)
        tv = iv("t")
        W("scalar", "t", tv)
        em("scalar", lambda eng, tt=tt, dt0=dt0, pp=pp: eng.copy(
            xnT_sb[:, dt0 * T + tt * 128: dt0 * T + tt * 128 + 128]
            if False else
            bass.AP(xnT_sb, dt0 * T + tt * 128, [[DT * T, 128], [T, 4], [1, 128]]),
            bass.AP(ps_all, 2048 + pp * 512, [[4096, 128], [128, 4], [1, 128]])
        ).then_inc(sem["a"], 1))
        tpg_evac.append(iv("a"))
    phXL.close()

    phAT = open_stack()
    attTf = phAT.enter_context(nc.sbuf_tensor("attTf", [128, DT * 512], BF16, side="right"))

    vunits = []
    for si in range(NPAIR):
        for j in range(4):
            vunits.append((w1, (si * 4 + j) * 128, 0, D, 0, q1))
        for j in range(4):
            vunits.append((w2, (si * 4 + j) * 128, 0, D, 1, q2))
    for j3 in range(DT):
        for (c0, wdt) in W3C:
            vunits.append((w3, j3 * 128, c0, wdt, 2, q3))
    NVU = len(vunits)
    v_q = [0] * NVU
    tok_ql = [None] * NVU
    tok_qw = [None] * NVU
    qlsem = ["dQl0", "dQl1"]
    qwsem = ["dQw0", "dQw1"]
    vq_iter = iter(range(NVU))

    def mk_qwrite(ui):
        wd, r0, c0, wdt, cidx, qd = vunits[ui]
        qp = ui % 2
        return lambda eng, qd=qd, r0=r0, c0=c0, wdt=wdt, qp=qp: eng.dma_start(
            qd[r0:r0 + 128, c0:c0 + wdt], qt_sb[:, qp * D: qp * D + wdt])

    def emit_va(count):
        for _ in range(count):
            try:
                ui = next(vq_iter)
            except StopIteration:
                return
            wd, r0, c0, wdt, cidx, qd = vunits[ui]
            sl = (ui % 2) * D
            W("scalar", "v", v_gred[NWT - 1] if ui < 2 else v_q[ui - 2])
            tok_ql[ui] = dma("scalar", qlsem[ui % 2],
                             lambda eng, wd=wd, r0=r0, c0=c0, wdt=wdt, sl=sl: eng.dma_start(
                                 ring[:, sl:sl + wdt], wd[r0:r0 + 128, c0:c0 + wdt]))
            if ui >= 1:
                W("scalar", "v", v_q[ui - 1])
                tok_qw[ui - 1] = dma("scalar", qwsem[(ui - 1) % 2], mk_qwrite(ui - 1))
            WT("vector", tok_ql[ui])
            if ui == 0:
                W("vector", "v", v_cready)
            qp = ui % 2
            if ui >= 2:
                WT("vector", tok_qw[ui - 2])
            if ui >= 1:
                W("vector", "v", v_q[ui - 1])
            em("vector", lambda eng, sl=sl, wdt=wdt, cidx=cidx: eng.tensor_single_scalar(
                lt_sb[:, 0:wdt], ring[:, sl:sl + wdt], ncb2[:, cidx:cidx + 1],
                ALU.is_lt).then_inc(sem["v"], 1))
            W("vector", "v", iv("v"))
            em("vector", lambda eng, sl=sl, wdt=wdt, cidx=cidx, qp=qp: eng.scalar_tensor_tensor(
                qt_sb[:, qp * D: qp * D + wdt], ring[:, sl:sl + wdt], cb2[:, cidx:cidx + 1],
                lt_sb[:, 0:wdt], ALU.is_gt, ALU.subtract).then_inc(sem["v"], 1))
            v_q[ui] = iv("v")
            if ui == NVU - 1:
                W("scalar", "v", v_q[ui])
                tok_qw[ui] = dma("scalar", qwsem[ui % 2], mk_qwrite(ui))

    emit_va(8)

    tok_xni = [None, None]

    def emit_xni(hf):
        if hf == 0:
            W("sync", "dMi", 16)
            W("sync", "c", c_xn)
        def load_xni(eng, hf=hf):
            with eng.register("rio%d" % hf) as rio:
                eng.reg_load(rio, ioffs[0:1, hf:hf + 1])
                off = eng.snap(rio)
                srcx = cc_xn_out.ap().rearrange("(s p) c -> p s c", p=128)
                return eng.dma_start(
                    bass.AP(xn_i, 0, [[NXT * 256, 128], [256, NXT], [1, 256]]),
                    srcx[:, :, bass.ds(off, 256)])
        tok_xni[hf] = dma("sync", "dXI", load_xni)

    ps_x = [ps_all[:, 0:1024], ps_all[:, 1024:2048]]
    fst = {"unit": 0, "ev": [], "win": 0, "wins": {}, "hid_tok": [None] * HT,
           "mulv": [0] * HT, "sig_a": [0] * HT}
    wlsem = ["dWa", "dWb"]
    hsem = ["dH0", "dH1", "dH2", "dH3"]
    tokq = {0: [None] * HT, 1: [None] * HT, 2: [None] * DT}

    def fill_tokmaps():
        for ui in range(NVU):
            if tok_qw[ui] is None:
                continue
            wd, r0, c0, wdt, cidx, qd = vunits[ui]
            if cidx < 2:
                tokq[cidx][r0 // 128] = tok_qw[ui]
            else:
                tokq[2][r0 // 128] = tok_qw[ui]

    def ffn_window(qd, cidx, si):
        fill_tokmaps()
        wi = fst["win"]
        buf = (wi % 2) * DT * SP
        lt_i = si * 4 + 3
        WT("sync", tokq[cidx][lt_i])
        WT("sync", tokq[cidx][lt_i - 1])
        if wi >= 2:
            W("sync", "t", fst["wins"][wi - 2])
        last_tok = None
        h0 = si * SP
        for K in range(DT):
            def ldw(eng, qd=qd, h0=h0, K=K, buf=buf):
                return eng.dma_start(
                    bass.AP(wq_sb, buf + K * SP, [[2 * DT * SP, 128], [1, SP]]),
                    qd[h0:h0 + SP, K * 128:(K + 1) * 128],
                    transpose=True)
            last_tok = dma("sync", wlsem[wi % 2], ldw)
        fst["win"] += 1
        return wi, buf, last_tok

    def ffn_half(si, kind, va_plan, prewin=None):
        cidx = 0 if kind == "sig" else 1
        qd = q1 if kind == "sig" else q2
        wi, buf, wtok = prewin if prewin is not None else ffn_window(qd, cidx, si)
        guard_a = VAL["a"]
        WT("tensor", wtok)
        W("tensor", "a", guard_a)
        emit_va(va_plan[0])
        for mt in range(NSLAB):
            slab = si * 4 + mt
            unit = fst["unit"]
            pp = unit % 2
            if unit >= 2:
                sname, sval = fst["ev"][unit - 2]
                W("tensor", sname, sval)
            for K in range(DT):
                for c in range(2):
                    def mmf(eng, buf=buf, K=K, mt=mt, c=c, pp=pp):
                        return eng.matmul(
                            ps_x[pp][:, c * 512:(c + 1) * 512],
                            wq_sb[:, buf + K * SP + mt * 128: buf + K * SP + mt * 128 + 128],
                            xnT_sb[:, K * T + c * 512: K * T + (c + 1) * 512],
                            start=(K == 0), stop=(K == DT - 1))
                    if K == DT - 1 and c == 1:
                        em("tensor", lambda eng, mmf=mmf: mmf(eng).then_inc(sem["t"], 1))
                    else:
                        em("tensor", mmf)
            tmm = iv("t")
            if mt == NSLAB - 1:
                fst["wins"][wi] = tmm
            sw = (si % 2) * NSLAB * T + mt * T
            if kind == "sig":
                W("scalar", "t", tmm)
                if si >= 2:
                    W("scalar", "v", fst["mulv"][slab - 2 * NSLAB])
                em("scalar", lambda eng, pp=pp, sw=sw: eng.activation(
                    sstg[:, sw:sw + T], ps_x[pp][:, :],
                    AF.Sigmoid).then_inc(sem["a"], 1))
                av = iv("a")
                fst["sig_a"][slab] = av
                fst["ev"].append(("a", av))
            else:
                W("vector", "t", tmm)
                W("vector", "a", fst["sig_a"][slab])
                hs = (slab % 4) * T
                if slab >= 4 and fst["hid_tok"][slab - 4] is not None:
                    WT("vector", fst["hid_tok"][slab - 4])
                em("vector", lambda eng, pp=pp, sw=sw, hs=hs: eng.tensor_tensor(
                    hstg[:, hs:hs + T], sstg[:, sw:sw + T],
                    ps_x[pp][:, :], ALU.mult).then_inc(sem["v"], 1))
                vv = iv("v")
                fst["mulv"][slab] = vv
                fst["ev"].append(("v", vv))
                W("gpsimd", "v", vv)
                fst["hid_tok"][slab] = dma(
                    "gpsimd", hsem[slab % 4], lambda eng, slab=slab, hs=hs: eng.dma_start(
                        hid_d[slab * 128:(slab + 1) * 128, :], hstg[:, hs:hs + T]))
            fst["unit"] += 1
            emit_va(va_plan[1])
        emit_va(va_plan[2])

    ps_g = [ps_all[:, 0:2048], ps_all[:, 2048:4096]]
    a_exp = [0] * NIT
    v_att = [0] * NIT
    t_att_last = [0] * NIT
    piece_tok = [None] * NIT
    t_ring_tok = {}
    atp_evac = [0, 0]

    def emit_gram_pass(ph):
        WT("tensor", tok_xni[ph])
        if ph == 0:
            ev = fst["ev"]
            W("tensor", ev[-1][0], ev[-1][1])
            W("tensor", ev[-2][0], ev[-2][1])
            W("tensor", "a", tpg_evac[-1])
            W("tensor", "a", tpg_evac[-2])
        else:
            W("tensor", "a", a_exp[1])
            W("tensor", "a", atp_evac[0])
            W("tensor", "a", atp_evac[1])
        for cc in range(NXT // 2):
            sl = (cc % 2) * 2 * D
            if (ph, cc - 2) in t_ring_tok:
                W("sync", "t", t_ring_tok[(ph, cc - 2)])
            elif ph > 0 and cc < 2:
                W("sync", "t", t_ring_tok[(0, NXT // 2 - 2 + cc)])
            tok = dma("sync", nsem[cc % 2], lambda eng, cc=cc, sl=sl: eng.dma_start(
                bass.AP(nring, sl, [[2 * 2 * D, 128], [D, 2], [1, D]]),
                cc_xn_out.ap().rearrange("(c p) d -> p c d", p=128)[:, 2 * cc:2 * cc + 2, :]))
            WT("tensor", tok)
            for i2 in range(2):
                ct = 2 * cc + i2
                for mh in range(2):
                    for c4 in range(4):
                        st, sp = (ct == 0), (ct == NXT - 1)
                        last = (i2 == 1 and mh == 1 and c4 == 3)
                        def mmg(eng, ph=ph, mh=mh, ct=ct, c4=c4, sl=sl, i2=i2, st=st, sp=sp):
                            return eng.matmul(
                                ps_g[mh][:, c4 * 512:(c4 + 1) * 512],
                                xn_i[:, ct * 256 + mh * 128:
                                     ct * 256 + mh * 128 + 128],
                                nring[:, sl + i2 * D + c4 * 512: sl + i2 * D + (c4 + 1) * 512],
                                start=st, stop=sp)
                        if last:
                            em("tensor", lambda eng, mmg=mmg: mmg(eng).then_inc(sem["t"], 1))
                        else:
                            em("tensor", mmg)
            t_ring_tok[(ph, cc)] = iv("t")

    def emit_softmax_attT(ph, mh):
        it = 2 * ph + mh
        t_gram = t_ring_tok[(ph, NXT // 2 - 1)]
        W("vector", "t", t_gram)
        em("vector", lambda eng, mh=mh: eng.reduce_max(
            vstat[:, 4:5], ps_g[mh][:, :], AX.X, negate=True).then_inc(sem["v"], 1))
        v_max = iv("v")
        W("scalar", "v", v_max)
        if it > 0:
            W("scalar", "v", v_att[it - 1])
            WT("scalar", piece_tok[it - 1])
        em("scalar", lambda eng, mh=mh: eng.activation(
            scratch[:, :], ps_g[mh][:, :], AF.Exp,
            bias=vstat[:, 4:5], accum_out=astat[:, 2:3]).then_inc(sem["a"], 1))
        a_exp[it] = iv("a")
        W("vector", "a", a_exp[it])
        em("vector", lambda eng: eng.reciprocal(
            vstat[:, 5:6], astat[:, 2:3]).then_inc(sem["v"], 1))
        W("vector", "v", iv("v"))
        if it > 0:
            W("vector", "t", t_att_last[it - 1])
        em("vector", lambda eng: eng.tensor_scalar_mul(
            att_sb[:, 0:D], scratch[:, :], vstat[:, 5:6]).then_inc(sem["v"], 1))
        v_att[it] = iv("v")

        W("tensor", "v", v_att[it])
        W("tensor", "a", a_exp[it])
        ps_tpg = [ps_g[mh][:, 0:512], ps_g[mh][:, 512:1024]]
        for gq in range(4):
            pp = gq % 2
            if gq >= 2:
                W("tensor", "a", atp_evac[pp])
            for j in range(4):
                jt = gq * 4 + j
                last = (j == 3)
                def mmat(eng, jt=jt, j=j, pp=pp):
                    return eng.matmul(
                        ps_tpg[pp][:, j * 128:(j + 1) * 128],
                        att_sb[:, jt * 128: jt * 128 + 128],
                        ident_sb[:, :], start=True, stop=True)
                if last:
                    em("tensor", lambda eng, mmat=mmat: mmat(eng).then_inc(sem["t"], 1))
                else:
                    em("tensor", mmat)
            tv = iv("t")
            W("scalar", "t", tv)
            em("scalar", lambda eng, gq=gq, pp=pp, mh=mh: eng.copy(
                scratch[:, gq * 512:(gq + 1) * 512],
                ps_g[mh][:, pp * 512:(pp + 1) * 512]).then_inc(sem["a"], 1))
            atp_evac[pp] = iv("a")
        t_att_last[it] = VAL["t"]
        W("scalar", "a", atp_evac[0])
        W("scalar", "a", atp_evac[1])
        piece_tok[it] = dma("scalar", "dB0" if it % 2 == 0 else "dB1",
                            lambda eng, it=it: eng.dma_start(
                                bass.AP(cc_att_in, it * 128, [[512, 128], [128 * 512, DT], [1, 128]]),
                                bass.AP(scratch, 0, [[D, 128], [128, DT], [1, 128]])))
        emit_va(4)

    for si in range(3):
        ffn_half(si, "sig", (2, 1, 0))
        ffn_half(si, "mul", (2, 0, 0))
    pre_w3 = ffn_window(q1, 0, 3)
    emit_xni(0)
    emit_gram_pass(0)
    emit_softmax_attT(0, 0)
    emit_softmax_attT(0, 1)
    W("sync", "t", t_ring_tok[(0, NXT // 2 - 1)])
    emit_xni(1)
    emit_gram_pass(1)
    emit_softmax_attT(1, 0)
    emit_softmax_attT(1, 1)
    ffn_half(3, "sig", (2, 1, 0), prewin=pre_w3)
    ffn_half(3, "mul", (2, 0, 0))
    for si in range(4, 6):
        ffn_half(si, "sig", (2, 1, 0))
        ffn_half(si, "mul", (2, 0, 0))
    W("gpsimd", "dB0", VAL["dB0"])
    W("gpsimd", "dB1", VAL["dB1"])
    em("gpsimd", lambda eng: eng.collective_compute(
        "AllGather", ALU.bypass,
        replica_groups=[[0, 1, 2, 3], [4, 5, 6, 7]],
        ins=[cc_att_in.ap().opt()], outs=[cc_att_out.ap().opt()]).then_inc(sem["c"]))
    c_att = iv("c")
    for si in range(6, 9):
        ffn_half(si, "sig", (2, 1, 0))
        ffn_half(si, "mul", (2, 0, 0))

    atsem = ["dAT0", "dAT1"]
    ps_o = [ps_all[:, 2048:2560], ps_all[:, 2560:3072]]
    asem = ["dA0", "dA1"]
    t_ao_blk = [0] * 4
    a_aoe = {}
    tok_ao = {}
    ao_seq = []

    def emit_ao_block(r):
        if r == 0:
            W("sync", "c", c_att)
        else:
            W("sync", "t", t_ao_blk[r - 1])
        atok = dma("sync", atsem[r % 2], lambda eng, r=r: eng.dma_start(
            bass.AP(attTf, 0, [[DT * 512, 128], [512, DT], [1, 512]]),
            bass.AP(cc_att_out, r * D * 512, [[512, 128], [128 * 512, DT], [1, 512]])))
        if r == 0:
            W("tensor", "a", a_exp[NIT - 1])
            W("tensor", "a", atp_evac[0])
            W("tensor", "a", atp_evac[1])
        WT("tensor", atok)
        for tt in range(NLT):
            pp = tt % 2
            key = (r, tt)
            if len(ao_seq) >= 2:
                W("tensor", "a", a_aoe[ao_seq[-2]])
            for jt in range(DT):
                st, sp = (jt == 0), (jt == DT - 1)
                def mmo(eng, tt=tt, jt=jt, pp=pp, st=st, sp=sp):
                    return eng.matmul(
                        ps_o[pp][:, 0:512],
                        xnT_sb[:, jt * T + tt * 128: jt * T + tt * 128 + 128],
                        attTf[:, jt * 512:(jt + 1) * 512],
                        start=st, stop=sp)
                if sp:
                    em("tensor", lambda eng, mmo=mmo: mmo(eng).then_inc(sem["t"], 1))
                else:
                    em("tensor", mmo)
            t_ao = iv("t")
            W("scalar", "t", t_ao)
            if len(ao_seq) >= 2:
                WT("scalar", tok_ao[ao_seq[-2]])
            em("scalar", lambda eng, pp=pp: eng.copy(
                stage[:, pp * 512:(pp + 1) * 512], ps_o[pp][:, 0:512]).then_inc(sem["a"], 1))
            a_aoe[key] = iv("a")
            W("gpsimd", "a", a_aoe[key])
            if not ao_seq:
                WT("gpsimd", tok_outinit)
            tok_ao[key] = dma("gpsimd", asem[pp], lambda eng, r=r, tt=tt, pp=pp: eng.dma_start(
                out[tt * 128:(tt + 1) * 128, r * 512:(r + 1) * 512],
                stage[:, pp * 512:(pp + 1) * 512],
                accum_op=ALU.add))
            ao_seq.append(key)
        t_ao_blk[r] = VAL["t"]

    ffn_half(9, "sig", (2, 1, 0))
    ffn_half(9, "mul", (2, 0, 0))
    emit_ao_block(0)
    ffn_half(10, "sig", (2, 1, 2))
    emit_ao_block(1)
    ffn_half(10, "mul", (2, 0, 4))
    emit_ao_block(2)
    emit_va(NVU)
    emit_ao_block(3)
    fill_tokmaps()

    phA1.close()
    phAT.close()
    phF.close()
    abx.close()
    phC = open_stack()
    ec = phC.enter_context
    fstage = ec(nc.sbuf_tensor("fstage", [128, 2 * 512], F32, side="left"))
    hT_sb = ec(nc.sbuf_tensor("hT_sb", [128, HT * T], BF16, side="right"))
    w3T_sb = ec(nc.sbuf_tensor("w3T_sb", [128, 2 * 22 * 512], BF16, side="right"))
    ps_f = [ps_all[:, 0:512], ps_all[:, 512:1024]]

    W("sync", "v", v_q[NVU - 1])
    WT("sync", tok_qw[NVU - 1])
    WT("sync", tok_qw[NVU - 2])
    W("sync", "t", t_ao_blk[3])
    W("sync", "a", VAL["a"])

    hlsem = ["dHL0", "dHL1"]
    tok_hl = [None] * HT
    for k in range(HT):
        WT("sync", fst["hid_tok"][k])
        tok_hl[k] = dma("sync", hlsem[k % 2], lambda eng, k=k: eng.dma_start(
            hT_sb[:, k * T:(k + 1) * T], hid_d.ap().rearrange("(s p) t -> p s t", p=128)[:, k, :]))

    wsem3 = ["dWa", "dWb"]
    fsem = ["dF0", "dF1"]
    f_seq = []
    v_fe = {}
    tok_f = {}
    t_half = {}
    hwin = 0
    for db in range(4):
        dout0 = db * 512
        half_toks = []
        for hf in range(2):
            buf = (hwin % 2) * 22 * 512
            if hwin >= 2:
                W("sync", "t", t_half[hwin - 2])
            ltok = None
            for K in range(22):
                Ka = hf * 22 + K
                def ld3(eng, dout0=dout0, Ka=Ka, K=K, buf=buf):
                    return eng.dma_start(
                        bass.AP(w3T_sb, buf + K * 512, [[2 * 22 * 512, 128], [1, 512]]),
                        q3[dout0:dout0 + 512, Ka * 128:(Ka + 1) * 128],
                        transpose=True)
                ltok = dma("sync", wsem3[hwin % 2], ld3)
            half_toks.append((hwin, buf, ltok))
            hwin += 1
        for mt in range(NLT):
            pp = len(f_seq) % 2
            key = (db, mt)
            if len(f_seq) >= 2:
                W("tensor", "v", v_fe[f_seq[-2]])
            elif fst["ev"]:
                W("tensor", fst["ev"][-1][0], fst["ev"][-1][1])
                W("tensor", fst["ev"][-2][0], fst["ev"][-2][1])
            for hf in range(2):
                wih, buf, ltok = half_toks[hf]
                if mt == 0:
                    WT("tensor", ltok)
                    if db == 0 and hf == 0:
                        WT("tensor", tok_hl[HT - 1])
                        WT("tensor", tok_hl[HT - 2])
                for K in range(22):
                    Ka = hf * 22 + K
                    st = (Ka == 0)
                    sp = (Ka == HT - 1)
                    def mm3(eng, buf=buf, K=K, Ka=Ka, mt=mt, pp=pp, st=st, sp=sp):
                        return eng.matmul(
                            ps_f[pp][:, 0:512],
                            hT_sb[:, Ka * T + mt * 128: Ka * T + mt * 128 + 128],
                            w3T_sb[:, buf + K * 512: buf + (K + 1) * 512],
                            start=st, stop=sp)
                    if sp:
                        em("tensor", lambda eng, mm3=mm3: mm3(eng).then_inc(sem["t"], 1))
                    else:
                        em("tensor", mm3)
            t_f = iv("t")
            if mt == NLT - 1:
                t_half[half_toks[0][0]] = t_f
                t_half[half_toks[1][0]] = t_f
            W("vector", "t", t_f)
            if len(f_seq) >= 2:
                WT("vector", tok_f[f_seq[-2]])
            em("vector", lambda eng, pp=pp: eng.tensor_copy(
                fstage[:, pp * 512:(pp + 1) * 512], ps_f[pp][:, 0:512]).then_inc(sem["v"], 1))
            v_fe[key] = iv("v")
            W("gpsimd", "v", v_fe[key])
            if not f_seq:
                W("gpsimd", "dA0", VAL["dA0"])
                W("gpsimd", "dA1", VAL["dA1"])
            tok_f[key] = dma("gpsimd", fsem[pp], lambda eng, dout0=dout0, mt=mt, pp=pp:
                             eng.dma_start(
                                 out[mt * 128:(mt + 1) * 128, dout0:dout0 + 512],
                                 fstage[:, pp * 512:(pp + 1) * 512],
                                 accum_op=ALU.add))
            f_seq.append(key)

    for s in DMA_SEMS:
        W("sync", s, VAL[s])
        W("gpsimd", s, VAL[s])

    with nc.Block() as block:
        @block.sync
        def _(eng):
            for f in streams["sync"]:
                f(eng)

        @block.vector
        def _(eng):
            for f in streams["vector"]:
                f(eng)

        @block.scalar
        def _(eng):
            for f in streams["scalar"]:
                f(eng)

        @block.tensor
        def _(eng):
            for f in streams["tensor"]:
                f(eng)

        @block.gpsimd
        def _(eng):
            for f in streams["gpsimd"]:
                f(eng)

    phC.close()
    top.close()
    return nc


_EYE = np.eye(128, dtype=ml_dtypes.bfloat16)
_CONSTS = np.concatenate([
    np.full((128, 1), GBIAS, np.float32),
    np.full((128, 1), LNN, np.float32)], axis=1)


def make_in_maps(x, w1, w2, w3):
    in_maps = []
    for k in range(8):
        b, q = k // 4, k % 4
        t0 = q * 1024
        wslk = w1[k * HSL:(k + 1) * HSL] if k < 4 else w2[(k - 4) * HSL:(k - 3) * HSL]
        in_maps.append({
            "xloc": np.ascontiguousarray(x[b, t0:t0 + 1024], dtype=np.float32),
            "wsl": np.ascontiguousarray(wslk, dtype=np.float32),
            "w3sl": np.ascontiguousarray(w3[k * W3R:(k + 1) * W3R], dtype=np.float32),
            "w1": np.ascontiguousarray(w1, dtype=np.float32),
            "w2": np.ascontiguousarray(w2, dtype=np.float32),
            "w3": np.ascontiguousarray(w3, dtype=np.float32),
            "ident": _EYE,
            "consts": _CONSTS,
            "ioff": np.array([[q * 512, q * 512 + 256]], dtype=np.uint32),
        })
    return in_maps


def kernel(L0_attn_x, L0_ffn_w1, L0_ffn_w2, L0_ffn_w3):
    x = np.asarray(L0_attn_x, dtype=np.float32)
    w1 = np.asarray(L0_ffn_w1, dtype=np.float32)
    w2 = np.asarray(L0_ffn_w2, dtype=np.float32)
    w3 = np.asarray(L0_ffn_w3, dtype=np.float32)
    nc = build_nc()
    res = run_bass_kernel_spmd(nc, make_in_maps(x, w1, w2, w3), core_ids=list(range(8)))
    outp = np.empty((2, 4096, 2048), np.float32)
    for k, r in enumerate(res.results):
        outp[k // 4, (k % 4) * 1024:((k % 4) + 1) * 1024] = r["out"]
    return outp
